# version 17
# baseline (speedup 1.0000x reference)
import numpy as np
from contextlib import ExitStack

import concourse.bass as bass
import concourse.mybir as mybir
from concourse.bass_utils import run_bass_kernel_spmd

F32 = mybir.dt.float32
BF16 = mybir.dt.bfloat16
FP8 = mybir.dt.float8e4
AF = mybir.ActivationFunctionType
ALU = mybir.AluOpType
AX = mybir.AxisListType.X

S_FULL, S_OWN, D = 4096, 2048, 2048
NH, DH, NHO = 16, 128, 8
I_FFN = 5632
CSH = 20.0
ASC = float(1.0 / np.sqrt(np.float64(DH)))
ENGINES = ("sync", "tensor", "scalar", "vector", "gpsimd")


def build_nc():
    nc = bass.Bass("TRN2", target_bir_lowering=False)

    x_d = nc.dram_tensor("x", [S_FULL, D], F32, kind="ExternalInput").ap()
    wqk_d = nc.dram_tensor("wqk", [D, 2048], F32, kind="ExternalInput").ap()
    wv_d = nc.dram_tensor("wv", [D, D], F32, kind="ExternalInput").ap()
    wo_d = nc.dram_tensor("wo", [D, D], F32, kind="ExternalInput").ap()
    wg_d = nc.dram_tensor("wg", [I_FFN, D], F32, kind="ExternalInput").ap()
    wu_d = nc.dram_tensor("wu", [I_FFN, D], F32, kind="ExternalInput").ap()
    wd_d = nc.dram_tensor("wd", [D, I_FFN], F32, kind="ExternalInput").ap()
    sel_d = nc.dram_tensor("sel", [128, 16], F32, kind="ExternalInput").ap()
    out_d = nc.dram_tensor("out", [S_OWN, D], F32, kind="ExternalOutput").ap()

    xn16_d = nc.dram_tensor("xn16_s", [S_FULL, D], BF16).ap()
    xq_d = nc.dram_tensor("xq_s", [S_OWN, D], BF16).ap()
    weffg_d = nc.dram_tensor("weffg_s", [I_FFN, D], BF16).ap()
    weffu_d = nc.dram_tensor("weffu_s", [I_FFN, D], BF16).ap()
    weffd_d = nc.dram_tensor("weffd_s", [D, I_FFN], BF16).ap()
    wgT_d = nc.dram_tensor("wgT_s", [D, I_FFN], BF16).ap()
    wuT_d = nc.dram_tensor("wuT_s", [D, I_FFN], BF16).ap()
    wdT_d = nc.dram_tensor("wdT_s", [I_FFN, D], BF16).ap()
    qt_d = nc.dram_tensor("qt_s", [S_OWN, I_FFN], BF16).ap()
    sat_d = nc.dram_tensor("sat_s", [S_OWN, D], BF16).ap()
    cc_in = nc.dram_tensor("cc_in", [16, S_OWN], F32).ap()
    c_dram = nc.dram_tensor("c_dram", [16, S_OWN], F32).ap()
    cc_out = nc.dram_tensor("cc_out", [128, S_OWN], F32).ap()

    ctr = {}
    cp = {}
    prog = {k: [] for k in ENGINES}

    es_top = ExitStack()
    es_top.__enter__()

    def sbuf(stack, name, shape, dt, side=None):
        return stack.enter_context(nc.sbuf_tensor(name, shape, dt, side=side))

    ones_f = sbuf(es_top, "ones_f", [128, 1], F32)
    cbias = sbuf(es_top, "cbias", [128, 2], F32)
    ones_r = sbuf(es_top, "ones_r", [1, 128], F32)
    sel_sb = sbuf(es_top, "sel_sb", [128, 16], F32)
    crow = sbuf(es_top, "crow", [1, S_FULL], F32)

    psA = es_top.enter_context(nc.psum_tensor("psA", [128, 2048], F32))
    psB = es_top.enter_context(nc.psum_tensor("psB", [128, 2048], F32))
    psb = [psA[:, i * 512:(i + 1) * 512] for i in range(4)] + \
          [psB[:, i * 512:(i + 1) * 512] for i in range(4)]

    sems = {}
    for s in ("sT", "sS", "sV", "sG", "sC", "bG",
              "dSel", "dX0", "dX1", "dPW0", "dPW1", "dPW2", "dPW3", "dTW0", "dTW1", "dXT",
              "dOT0", "dOT1", "dQX0", "dQX1", "dWG0", "dWG1", "dQT", "dWD",
              "dCB0", "dCB1", "dO0", "dO1",
              "dN0", "dN1", "dXQ0", "dXQ1", "dWS0", "dWS1", "dTS0", "dTS1",
              "dQK0", "dQK1", "dCC", "dWV0", "dWV1", "dCS", "dCD",
              "dC10", "dC11", "dWO0", "dWO1", "dSA0", "dSA1", "dQW0", "dQW1"):
        sems[s] = es_top.enter_context(nc.semaphore(s))

    def op(eng, fn, waits=(), inc=None, inc_amt=1, name=None):
        wl = []
        for w in waits:
            if w is None:
                continue
            if isinstance(w, str):
                wl.append(cp[w])
            else:
                wl.append(w)
        if inc is not None:
            ctr[inc] = ctr.get(inc, 0) + inc_amt
            val = ctr[inc]
        def run(e, fn=fn, wl=tuple(wl), inc=inc, inc_amt=inc_amt):
            for (s, v) in wl:
                if v > 0:
                    e.wait_ge(sems[s], v)
            inst = fn(e)
            if inc is not None:
                if inst is None:
                    inst = e.nop()
                inst.then_inc(sems[inc], inc_amt)
        prog[eng].append(run)
        if name is not None:
            assert inc is not None
            cp[name] = (inc, val)

    def mark(name, sem):
        cp[name] = (sem, ctr.get(sem, 0))

    block = es_top.enter_context(nc.Block())

    def flush():
        for name in ENGINES:
            lst = prog[name]
            if not lst:
                continue
            prog[name] = []
            getattr(block, name)(lambda e, lst=tuple(lst): [f(e) for f in lst])

    def barrier(tag):
        targets = [(s, ctr.get(s, 0)) for s in sems
                   if s[0] == "d" and ctr.get(s, 0) > 0]
        for name in ENGINES:
            op(name, lambda e: None, waits=targets)
        for name in ("tensor", "scalar", "vector", "gpsimd"):
            op(name, lambda e: e.drain(fusable=False))
        for name in ENGINES:
            op(name, lambda e: None, inc="bG")
        arrive = ctr["bG"]
        for name in ENGINES:
            op(name, lambda e: None, waits=[("bG", arrive)])
        flush()

    HW = 2048
    esPW = ExitStack(); esPW.__enter__()
    wqf2 = [sbuf(esPW, "wqf%d" % i, [128, HW], F32) for i in range(3)]
    wq8 = sbuf(esPW, "wq8", [128, HW], FP8)
    wqb2 = [sbuf(esPW, "wqb%d" % i, [128, HW], BF16) for i in range(2)]
    wtb2 = [sbuf(esPW, "wtb%d" % i, [128, HW], BF16) for i in range(2)]
    wsc = sbuf(esPW, "wsc", [128, 8], F32)
    wscd = sbuf(esPW, "wscd", [128, 32], F32)


    pw_thunks = []
    tw_thunks = []

    def make_qw(tag, src_d, dst_d, nchunks, width, prevtag, nprev):
        for k in range(nchunks):
            def thunk(k=k, tag=tag, src_d=src_d, dst_d=dst_d, width=width,
                      prevtag=prevtag, nprev=nprev):
                wf, wb = wqf2[k % 3], wqb2[k % 2]
                op("sync", lambda e, k=k, wf=wf: e.dma_start(
                    out=wf[:, :width], in_=src_d[k * 128:(k + 1) * 128, :]),
                   waits=[("%s.q8.%d" % (tag, k - 3)) if k >= 3 else
                          (("%s.q8.last%d" % (prevtag, k)) if prevtag else None)],
                   inc="dPW%d" % (k % 4), inc_amt=16, name="%s.ld.%d" % (tag, k))
                op("vector", lambda e, wf=wf: e.reduce_max(
                    wsc[:, 0:1], wf[:, :width], axis=AX, apply_absolute_value=True),
                   waits=["%s.ld.%d" % (tag, k),
                          ("%s.bf.%d" % (tag, k - 1)) if k >= 1 else
                          (("%s.bf.last" % prevtag) if prevtag else None)],
                   inc="sV", name="%s.m.%d" % (tag, k))
                op("vector", lambda e: e.tensor_scalar_mul(wsc[:, 1:2], wsc[:, 0:1], 1.0 / 224.0),
                   waits=["%s.m.%d" % (tag, k)], inc="sV", name="%s.msc.%d" % (tag, k))
                op("vector", lambda e: e.reciprocal(wsc[:, 2:3], wsc[:, 1:2]),
                   waits=["%s.msc.%d" % (tag, k)], inc="sV", name="%s.minv.%d" % (tag, k))
                op("scalar", lambda e, wf=wf: e.activation(
                    out=wq8[:, :width], in_=wf[:, :width], func=AF.Copy, scale=wsc[:, 2:3]),
                   waits=["%s.minv.%d" % (tag, k),
                          ("%s.bf.%d" % (tag, k - 1)) if k >= 1 else
                          (("%s.bf.last" % prevtag) if prevtag else None)],
                   inc="sS", name="%s.q8.%d" % (tag, k))
                op("scalar", lambda e, wb=wb: e.activation(
                    out=wb[:, :width], in_=wq8[:, :width], func=AF.Copy, scale=wsc[:, 1:2]),
                   waits=["%s.q8.%d" % (tag, k),
                          ("%s.st.%d" % (tag, k - 2)) if k >= 2 else
                          (("%s.st.%d" % (prevtag, nprev - 2 + k)) if prevtag else None)],
                   inc="sS", name="%s.bf.%d" % (tag, k))
                op("gpsimd", lambda e, k=k, wb=wb: e.dma_start(
                    out=dst_d[k * 128:(k + 1) * 128, :], in_=wb[:, :width]),
                   waits=["%s.bf.%d" % (tag, k)], inc="dWS%d" % (k % 2), inc_amt=16,
                   name="%s.st.%d" % (tag, k))
                if k == nchunks - 1:
                    cp["%s.bf.last" % tag] = cp["%s.bf.%d" % (tag, k)]
                    cp["%s.q8.last" % tag] = cp["%s.q8.%d" % (tag, k)]
                    for kk in range(6):
                        b = kk % 3
                        lastk = nchunks - 1 - ((nchunks - 1 - b) % 3)
                        cp["%s.q8.last%d" % (tag, kk)] = cp["%s.q8.%d" % (tag, lastk)]
            pw_thunks.append(thunk)

    def make_qw_scanquant(tag, src_d, dst_d, nchunks, width, prevtag, nprev):
        pieces = []
        w0 = 0
        while w0 < width:
            pieces.append((w0, min(width, w0 + HW)))
            w0 += HW
        npc = len(pieces)
        flc = [0]

        def scan_thunk(k):
            def thunk(k=k):
                for p, (a0, a1) in enumerate(pieces):
                    fl = flc[0]; flc[0] += 1
                    op("sync", lambda e, k=k, a0=a0, a1=a1, fl=fl: e.dma_start(
                        out=wqf2[fl % 3][:, :a1 - a0],
                        in_=src_d[k * 128:(k + 1) * 128, a0:a1]),
                       waits=[("%s.used.%d" % (tag, fl - 3)) if fl >= 3 else
                              (("%s.q8.last%d" % (prevtag, fl)) if prevtag else None)],
                       inc="dPW%d" % (fl % 4), inc_amt=16, name="%s.ld.%d" % (tag, fl))
                    op("vector", lambda e, p=p, a0=a0, a1=a1, fl=fl: e.reduce_max(
                        wsc[:, p:p + 1], wqf2[fl % 3][:, :a1 - a0],
                        axis=AX, apply_absolute_value=True),
                       waits=["%s.ld.%d" % (tag, fl),
                              ("%s.sc.%d" % (tag, k - 1)) if (p == 0 and k >= 1) else
                              (("%s.bf.last" % prevtag) if (p == 0 and k == 0 and prevtag) else None)],
                       inc="sV", name="%s.used.%d" % (tag, fl))
                op("vector", lambda e: e.tensor_max(wsc[:, 4:5], wsc[:, 0:1], wsc[:, 1:2]),
                   waits=["%s.used.%d" % (tag, flc[0] - 1)],
                   inc="sV", name="%s.mm1.%d" % (tag, k))
                op("vector", lambda e, k=k: e.tensor_tensor(
                    out=wsc[:, 5:6], in0=wsc[:, 4:5], in1=wsc[:, 2:3], op=ALU.max),
                   waits=["%s.mm1.%d" % (tag, k)],
                   inc="sV", name="%s.mred.%d" % (tag, k))
                op("vector", lambda e, k=k: e.tensor_scalar_mul(
                    wscd[:, 2 * k:2 * k + 1], wsc[:, 5:6], 1.0 / 224.0),
                   waits=["%s.mred.%d" % (tag, k)],
                   inc="sV", name="%s.msc2.%d" % (tag, k))
                op("vector", lambda e, k=k: e.reciprocal(
                    wscd[:, 2 * k + 1:2 * k + 2], wscd[:, 2 * k:2 * k + 1]),
                   waits=["%s.msc2.%d" % (tag, k)],
                   inc="sV", name="%s.sc.%d" % (tag, k))
            return thunk

        def quant_thunk(k):
            def thunk(k=k):
                for p, (a0, a1) in enumerate(pieces):
                    fl = flc[0]; flc[0] += 1
                    op("sync", lambda e, k=k, a0=a0, a1=a1, fl=fl: e.dma_start(
                        out=wqf2[fl % 3][:, :a1 - a0],
                        in_=src_d[k * 128:(k + 1) * 128, a0:a1]),
                       waits=[("%s.used.%d" % (tag, fl - 3)) if fl >= 3 else None],
                       inc="dPW%d" % (fl % 4), inc_amt=16, name="%s.ld.%d" % (tag, fl))
                    op("scalar", lambda e, k=k, a0=a0, a1=a1, fl=fl: e.activation(
                        out=wq8[:, :a1 - a0], in_=wqf2[fl % 3][:, :a1 - a0],
                        func=AF.Copy, scale=wscd[:, 2 * k + 1:2 * k + 2]),
                       waits=["%s.ld.%d" % (tag, fl), "%s.sc.%d" % (tag, k),
                              ("%s.bf.%d.%d" % (tag, k, p - 1)) if p >= 1 else
                              (("%s.bf.%d.last" % (tag, k - 1)) if k >= 1 else
                               (("%s.bf.last" % prevtag) if prevtag else None))],
                       inc="sS", name="%s.used.%d" % (tag, fl))
                    op("scalar", lambda e, k=k, p=p, a0=a0, a1=a1: e.activation(
                        out=wqb2[p % 2][:, :a1 - a0], in_=wq8[:, :a1 - a0],
                        func=AF.Copy, scale=wscd[:, 2 * k:2 * k + 1]),
                       waits=["%s.used.%d" % (tag, fl),
                              ("%s.st.%d.%d" % (tag, k, p - 2)) if p >= 2 else
                              (("%s.st.%d.%d" % (tag, k - 1, 2 if p == 0 else 1)) if k >= 1 else
                               (("%s.st.%d" % (prevtag, nprev - 2 + (p % 2))) if prevtag else None))],
                       inc="sS", name="%s.bf.%d.%d" % (tag, k, p))
                    op("gpsimd", lambda e, k=k, p=p, a0=a0, a1=a1: e.dma_start(
                        out=dst_d[k * 128:(k + 1) * 128, a0:a1], in_=wqb2[p % 2][:, :a1 - a0]),
                       waits=["%s.bf.%d.%d" % (tag, k, p)], inc="dWS%d" % (p % 2), inc_amt=16,
                       name="%s.st.%d.%d" % (tag, k, p))
                cp["%s.bf.%d.last" % (tag, k)] = cp["%s.bf.%d.%d" % (tag, k, npc - 1)]
                if k == nchunks - 1:
                    cp["%s.bf.last" % tag] = cp["%s.bf.%d.%d" % (tag, k, npc - 1)]
                    cp["pw.ad0"] = ("dWS0", ctr.get("dWS0", 0))
                    cp["pw.ad1"] = ("dWS1", ctr.get("dWS1", 0))
            return thunk

        sq = [scan_thunk(k) for k in range(nchunks)] + [quant_thunk(k) for k in range(nchunks)]
        out = []
        si, qi = 0, 0
        for k in range(2 * nchunks):
            if si < nchunks and si <= qi + 1:
                out.append(sq[si]); si += 1
            else:
                out.append(sq[nchunks + qi]); qi += 1
        pw_thunks.extend(out)

    make_qw("pwg", wg_d, weffg_d, I_FFN // 128, D, None, 0)
    make_qw("pwu", wu_d, weffu_d, I_FFN // 128, D, "pwg", 44)
    make_qw_scanquant("pwd", wd_d, weffd_d, D // 128, I_FFN, "pwu", 44)

    def make_tw(tag, src_d, dst_d, nchunks, width, previdx):
        nh = (width + HW - 1) // HW
        idx = [previdx]
        for k in range(nchunks):
            for hx in range(nh):
                w0 = hx * HW
                w1 = min(width, w0 + HW)
                fi = idx[0]
                idx[0] += 1
                def thunk(k=k, hx=hx, w0=w0, w1=w1, fi=fi, tag=tag,
                          src_d=src_d, dst_d=dst_d):
                    wt = wtb2[fi % 2]
                    op("sync", lambda e: e.dma_start(
                        out=wt[:, :w1 - w0],
                        in_=src_d[w0:w1, k * 128:(k + 1) * 128], transpose=True),
                       waits=["pw.ad0", "pw.ad1",
                              ("tw.st.%d" % (fi - 2)) if fi - 2 >= 0 else None],
                       inc="dTW%d" % (fi % 2), inc_amt=16, name="tw.ld.%d" % fi)
                    op("gpsimd", lambda e: e.dma_start(
                        out=dst_d[k * 128:(k + 1) * 128, w0:w1], in_=wt[:, :w1 - w0]),
                       waits=["tw.ld.%d" % fi], inc="dTS%d" % (fi % 2), inc_amt=16,
                       name="tw.st.%d" % fi)
                tw_thunks.append(thunk)
        return idx[0]

    _ti = make_tw("twg", weffg_d, wgT_d, D // 128, I_FFN, 0)
    _ti = make_tw("twu", weffu_d, wuT_d, D // 128, I_FFN, _ti)
    _ti = make_tw("twd", weffd_d, wdT_d, I_FFN // 128, D, _ti)

    def drain(lst, n):
        while n > 0 and lst:
            lst.pop(0)()
            n -= 1

    esA = ExitStack(); esA.__enter__()
    xt2 = [sbuf(esA, "xt%d" % i, [128, D], F32) for i in range(2)]
    cent2 = [sbuf(esA, "cent%d" % i, [128, D], F32) for i in range(2)]
    sqt = sbuf(esA, "sqt", [128, D], F32)
    xnf2 = [sbuf(esA, "xnf%d" % i, [128, D], F32) for i in range(2)]
    xn16_2 = [sbuf(esA, "xn16_%d" % i, [128, D], BF16) for i in range(2)]
    q8t2 = [sbuf(esA, "q8t%d" % i, [128, D], FP8) for i in range(2)]
    xeff2 = [sbuf(esA, "xeff%d" % i, [128, D], BF16) for i in range(2)]
    lnsm = sbuf(esA, "lnsm", [128, 16], F32)

    op("vector", lambda e: e.memset(ones_f[:], 1.0), inc="sV")
    op("vector", lambda e: e.memset(cbias[:, 0:1], 1e-5), inc="sV")
    op("vector", lambda e: e.memset(cbias[:, 1:2], -CSH), inc="sV")
    op("vector", lambda e: e.memset(ones_r[:], 1.0), inc="sV", name="init_ones")
    op("sync", lambda e: e.dma_start(out=sel_sb[:], in_=sel_d), inc="dSel", inc_amt=16, name="sel_ld")

    NT = S_FULL // 128
    NOWN = S_OWN // 128
    for it in range(NT):
        pi = it % 2
        own = it < NOWN
        xt, cent, xnf = xt2[pi], cent2[pi], xnf2[pi]
        xn16t, q8t, xeff = xn16_2[pi], q8t2[pi], xeff2[pi]
        def L(k, pi=pi):
            return lnsm[:, 8 * pi + k: 8 * pi + k + 1]
        op("sync", lambda e, it=it, xt=xt: e.dma_start(out=xt[:], in_=x_d[it * 128:(it + 1) * 128, :]),
           waits=[("P0.lastread.%d" % (it - 2)) if it >= 2 else None],
           inc="dX%d" % (it % 2), inc_amt=16, name="P0.xd.%d" % it)
        op("vector", lambda e, xt=xt, L=L: e.reduce_sum(L(0), xt[:], axis=AX),
           waits=["P0.xd.%d" % it, ("P0.lastread.%d" % (it - 2)) if it >= 2 else None],
           inc="sV", name="P0.su.%d" % it)
        op("vector", lambda e, L=L: e.tensor_scalar_mul(L(1), L(0), 1.0 / D),
           waits=["P0.su.%d" % it], inc="sV", name="P0.mu.%d" % it)
        op("vector", lambda e, xt=xt, cent=cent, L=L: e.tensor_scalar(
            out=cent[:], in0=xt[:], scalar1=L(1), scalar2=None, op0=ALU.subtract),
           waits=["P0.mu.%d" % it], inc="sV", name="P0.cent.%d" % it)
        op("scalar", lambda e, cent=cent, L=L: e.activation(
            out=sqt[:], in_=cent[:], func=AF.Square, accum_out=L(2)),
           waits=["P0.cent.%d" % it,
                  ("P0.std.%d" % (it - 2)) if it >= 2 else None],
           inc="sS", name="P0.sq.%d" % it)
        op("scalar", lambda e, L=L: e.activation(
            out=L(4), in_=L(2), func=AF.Sqrt, scale=1.0 / D, bias=cbias[:, 0:1]),
           waits=["P0.sq.%d" % it], inc="sS", name="P0.std.%d" % it)
        op("vector", lambda e, L=L: e.reciprocal(L(5), L(4)),
           waits=["P0.std.%d" % it], inc="sV", name="P0.rstd.%d" % it)
        op("scalar", lambda e, cent=cent, xn16t=xn16t, L=L: e.activation(
            out=xn16t[:], in_=cent[:], func=AF.Copy, scale=L(5)),
           waits=["P0.rstd.%d" % it, ("P0.xnd.%d" % (it - 2)) if it >= 2 else None],
           inc="sS", name="P0.xn16.%d" % it)
        if own:
            op("scalar", lambda e, cent=cent, xnf=xnf, L=L: e.activation(
                out=xnf[:], in_=cent[:], func=AF.Copy, scale=L(5)),
               inc="sS", name="P0.lastread.%d" % it)
        else:
            cp["P0.lastread.%d" % it] = cp["P0.xn16.%d" % it]
        op("gpsimd", lambda e, it=it, xn16t=xn16t: e.dma_start(
            out=xn16_d[it * 128:(it + 1) * 128, :], in_=xn16t[:]),
           waits=["P0.xn16.%d" % it], inc="dN%d" % (it % 2), inc_amt=16, name="P0.xnd.%d" % it)
        if own:
            op("vector", lambda e, xnf=xnf, L=L: e.reduce_max(
                L(6), xnf[:], axis=AX, apply_absolute_value=True),
               waits=["P0.lastread.%d" % it], inc="sV", name="P0.m.%d" % it)
            op("vector", lambda e, L=L: e.tensor_scalar_mul(L(7), L(6), 1.0 / 224.0),
               waits=["P0.m.%d" % it], inc="sV", name="P0.msc.%d" % it)
            op("vector", lambda e, L=L: e.reciprocal(L(3), L(7)),
               waits=["P0.msc.%d" % it], inc="sV", name="P0.minv.%d" % it)
            op("scalar", lambda e, xnf=xnf, q8t=q8t, L=L: e.activation(
                out=q8t[:], in_=xnf[:], func=AF.Copy, scale=L(3)),
               waits=["P0.minv.%d" % it,
                      ("P0.xe.%d" % (it - 2)) if it >= 2 else None],
               inc="sS", name="P0.q8.%d" % it)
            op("scalar", lambda e, q8t=q8t, xeff=xeff, L=L: e.activation(
                out=xeff[:], in_=q8t[:], func=AF.Copy, scale=L(7)),
               waits=["P0.q8.%d" % it, ("P0.xqd.%d" % (it - 2)) if it >= 2 else None],
               inc="sS", name="P0.xe.%d" % it)
            op("gpsimd", lambda e, it=it, xeff=xeff: e.dma_start(
                out=xq_d[it * 128:(it + 1) * 128, :], in_=xeff[:]),
               waits=["P0.xe.%d" % it], inc="dXQ%d" % (it % 2), inc_amt=16, name="P0.xqd.%d" % it)
        if it >= 2:
            drain(pw_thunks, 2)

    barrier("A")
    esA.close()


    esB1a = ExitStack(); esB1a.__enter__()
    xnT_own = sbuf(esB1a, "xnT_own", [128, 16 * S_OWN], BF16)
    esB1b = ExitStack(); esB1b.__enter__()
    qh2 = [sbuf(esB1b, "qh_sb%d" % i, [128, S_FULL], BF16, side="right") for i in range(2)]
    kh2 = [sbuf(esB1b, "kh_sb%d" % i, [128, S_FULL], BF16, side="right") for i in range(2)]
    e_row2 = [sbuf(esB1b, "e_row%d" % i, [128, S_FULL], BF16, side="right") for i in range(2)]
    c_acc = sbuf(esB1b, "c_acc", [128, S_FULL], F32, side="right")
    wqh = sbuf(esB1b, "wqh", [128, 2048], BF16, side="right")
    wkh = sbuf(esB1b, "wkh", [128, 2048], BF16, side="right")
    oth2 = [sbuf(esB1b, "oth%d" % i, [128, S_OWN], BF16, side="right") for i in range(2)]
    zrow = sbuf(esB1b, "zrow", [128, 8], F32, side="right")
    zsm = sbuf(esB1b, "zsm", [128, 8], F32, side="right")

    for ic in range(16):
        op("sync", lambda e, ic=ic: e.dma_start(
            out=xnT_own[:, ic * S_OWN:(ic + 1) * S_OWN],
            in_=xn16_d[0:S_OWN, ic * 128:(ic + 1) * 128], transpose=True),
           inc="dXT", inc_amt=16)
    mark("B1.xnT", "dXT")

    def b1_load_w(hh):
        op("gpsimd", lambda e, hh=hh: e.dma_start(
            out=wqh[:].rearrange("p (ic c) -> p ic c", c=128),
            in_=wqk_d[:, hh * 128:(hh + 1) * 128].rearrange("(ic p) c -> p ic c", p=128)),
           waits=[("P1.projmmdone.%d" % (hh - 1)) if hh >= 1 else None],
           inc="dQK0", inc_amt=16, name="P1.wq.%d" % hh)
        op("gpsimd", lambda e, hh=hh: e.dma_start(
            out=wkh[:].rearrange("p (ic c) -> p ic c", c=128),
            in_=wqk_d[:, 1024 + hh * 128:1024 + (hh + 1) * 128].rearrange("(ic p) c -> p ic c", p=128)),
           inc="dQK0", inc_amt=16, name="P1.wk.%d" % hh)

    def b1_proj(hh, extdeps):
        qdst, kdst = qh2[hh % 2], kh2[hh % 2]
        for ic in range(16):
            first_waits = []
            if ic == 0:
                first_waits = ["P1.wk.%d" % hh, "B1.xnT"] + extdeps
                if hh >= 2:
                    first_waits.append("P1.scmm.%d" % (hh - 2))
            nmm = 0
            for part in ("q", "k"):
                w_sb = wqh if part == "q" else wkh
                for tch in range(4):
                    bank = (0 if part == "q" else 4) + tch
                    nmm += 1
                    op("tensor", lambda e, w_sb=w_sb, ic=ic, tch=tch, bank=bank: e.matmul(
                        psb[bank], w_sb[:, ic * 128:(ic + 1) * 128],
                        xnT_own[:, ic * S_OWN + tch * 512: ic * S_OWN + tch * 512 + 512],
                        start=(ic == 0), stop=(ic == 15)),
                       waits=first_waits if nmm == 1 else [],
                       inc="sT" if (ic == 15 and nmm == 8) else None,
                       name=("P1.ownmm.%d" % hh) if (ic == 15 and nmm == 8) else None)
        for part in ("q", "k"):
            dst = qdst if part == "q" else kdst
            for tch in range(4):
                bank = (0 if part == "q" else 4) + tch
                op("scalar", lambda e, dst=dst, tch=tch, bank=bank: e.copy(
                    out=dst[:, tch * 512:tch * 512 + 512], in_=psb[bank]),
                   waits=["P1.ownmm.%d" % hh,
                          ("P1.scmm.%d" % (hh - 2)) if hh >= 2 else None],
                   inc="sS", name="P1.ev_%s.%d.%d" % (part, hh, tch))
        mark("P1.ownev.%d" % hh, "sS")
        for ic in range(16):
            oth = oth2[ic % 2]
            op("scalar", lambda e, hh=hh, ic=ic, oth=oth: e.dma_start(
                out=oth[:], in_=xn16_d[S_OWN:S_FULL, ic * 128:(ic + 1) * 128], transpose=True),
               waits=[("P1.othmm.%d.%d" % (hh, ic - 2)) if ic >= 2 else
                      (("P1.oevdone.%d" % (hh - 1)) if hh >= 1 else None)],
               inc="dOT%d" % (ic % 2), inc_amt=16, name="P1.oth.%d.%d" % (hh, ic))
            first_waits = ["P1.oth.%d.%d" % (hh, ic)]
            if ic == 0:
                first_waits.append("P1.ownev.%d" % hh)
            nmm = 0
            for part in ("q", "k"):
                w_sb = wqh if part == "q" else wkh
                for tch in range(4):
                    bank = (0 if part == "q" else 4) + tch
                    nmm += 1
                    op("tensor", lambda e, w_sb=w_sb, oth=oth, ic=ic, tch=tch, bank=bank: e.matmul(
                        psb[bank], w_sb[:, ic * 128:(ic + 1) * 128],
                        oth[:, tch * 512:tch * 512 + 512],
                        start=(ic == 0), stop=(ic == 15)),
                       waits=first_waits if nmm == 1 else [],
                       inc="sT" if nmm == 8 else None,
                       name=("P1.othmm.%d.%d" % (hh, ic)) if nmm == 8 else None)
        for part in ("q", "k"):
            dst = qdst if part == "q" else kdst
            for tch in range(4):
                bank = (0 if part == "q" else 4) + tch
                op("scalar", lambda e, dst=dst, tch=tch, bank=bank: e.copy(
                    out=dst[:, 2048 + tch * 512: 2048 + tch * 512 + 512], in_=psb[bank]),
                   waits=["P1.othmm.%d.15" % hh], inc="sS",
                   name="P1.oev_%s.%d.%d" % (part, hh, tch))
        mark("P1.oevdone.%d" % hh, "sS")
        mark("P1.qkdone.%d" % hh, "sS")
        mark("P1.projmmdone.%d" % hh, "sT")

    b1_load_w(0)
    b1_proj(0, [])

    for h in range(NHO):
        prev_cred_done = ("P1.cred.%d" % (h - 1)) if h >= 1 else None
        if h + 1 < NHO:
            b1_load_w(h + 1)
        qh_sb, kh_sb = qh2[h % 2], kh2[h % 2]
        for qt in range(32):
            qp = qt % 2
            e_row = e_row2[qp]
            zc = zrow[:, qp * 4: qp * 4 + 2]
            zs0 = zsm[:, qp * 4: qp * 4 + 1]
            zs1 = zsm[:, qp * 4 + 1: qp * 4 + 2]
            for kc in range(8):
                mmw = []
                if qt == 0 and kc == 0:
                    mmw.append("P1.qkdone.%d" % h)
                if qt >= 1 and kc % 4 == 0:
                    mmw.append("P1.ex%s.%d.%d" % ("A" if kc < 4 else "B", h, qt - 1))
                out_ps = psA[:, (kc % 4) * 512:(kc % 4) * 512 + 512] if kc < 4 else \
                         psB[:, (kc % 4) * 512:(kc % 4) * 512 + 512]
                nm = ("P1.sc%s.%d.%d" % ("A" if kc < 4 else "B", h, qt)) if kc % 4 == 3 else None
                op("tensor", lambda e, qt=qt, kc=kc, out_ps=out_ps, qh_sb=qh_sb, kh_sb=kh_sb: e.matmul(
                    out_ps, qh_sb[:, qt * 128:(qt + 1) * 128],
                    kh_sb[:, kc * 512:kc * 512 + 512], start=True, stop=True),
                   waits=mmw, inc="sT" if nm else None, name=nm)
                if kc == 3:
                    exw = ["P1.scA.%d.%d" % (h, qt)]
                    if qt >= 2:
                        exw += ["P1.cacc.%d.%d" % (h, qt - 2), "P1.z.%d.%d" % (h, qt - 2)]
                    elif h >= 1:
                        exw += ["P1.cacc.%d.31" % (h - 1), "P1.z.%d.31" % (h - 1)]
                    op("scalar", lambda e, e_row=e_row, zc=zc: e.activation(
                        out=e_row[:, 0:2048], in_=psA[:, :], func=AF.Exp,
                        scale=ASC, bias=cbias[:, 1:2], accum_out=zc[:, 0:1]),
                       waits=exw, inc="sS", name="P1.exA.%d.%d" % (h, qt))
                if kc == 7:
                    op("scalar", lambda e, e_row=e_row, zc=zc: e.activation(
                        out=e_row[:, 2048:4096], in_=psB[:, :], func=AF.Exp,
                        scale=ASC, bias=cbias[:, 1:2], accum_out=zc[:, 1:2]),
                       waits=["P1.scB.%d.%d" % (h, qt)], inc="sS",
                       name="P1.exB.%d.%d" % (h, qt))
            op("vector", lambda e, zc=zc, zs0=zs0: e.reduce_sum(zs0, zc, axis=AX),
               waits=["P1.exB.%d.%d" % (h, qt)], inc="sV", name="P1.zs.%d.%d" % (h, qt))
            op("vector", lambda e, zs0=zs0, zs1=zs1: e.reciprocal(zs1, zs0),
               waits=["P1.zs.%d.%d" % (h, qt)], inc="sV", name="P1.z.%d.%d" % (h, qt))
            if qt == 0:
                op("vector", lambda e, e_row=e_row, zs1=zs1: e.tensor_scalar(
                    out=c_acc[:], in0=e_row[:], scalar1=zs1, scalar2=None, op0=ALU.mult),
                   waits=["P1.z.%d.0" % h, prev_cred_done], inc="sV", name="P1.cacc.%d.0" % h)
            else:
                op("vector", lambda e, e_row=e_row, zs1=zs1: e.scalar_tensor_tensor(
                    out=c_acc[:], in0=e_row[:], scalar=zs1, in1=c_acc[:],
                    op0=ALU.mult, op1=ALU.add),
                   waits=["P1.z.%d.%d" % (h, qt)], inc="sV", name="P1.cacc.%d.%d" % (h, qt))
        mark("P1.scmm.%d" % h, "sT")
        for kc in range(8):
            op("tensor", lambda e, kc=kc: e.matmul(
                psB[0:1, 0:512], ones_f[:], c_acc[:, kc * 512:kc * 512 + 512],
                start=True, stop=True),
               waits=["P1.cacc.%d.31" % h, "init_ones",
                      ("P1.crow.%d.%d" % (h, kc - 1)) if kc >= 1 else None],
               inc="sT", name="P1.cr.%d.%d" % (h, kc))
            op("scalar", lambda e, kc=kc: e.copy(
                out=crow[0:1, kc * 512:kc * 512 + 512], in_=psB[0:1, 0:512]),
               waits=["P1.cr.%d.%d" % (h, kc),
                      ("P1.ccw.%d" % (h - 1)) if (kc == 0 and h >= 1) else None],
               inc="sS", name="P1.crow.%d.%d" % (h, kc))
        mark("P1.cred.%d" % h, "sT")
        mark("P1.crowdone.%d" % h, "sS")
        mark("P1.exdone.%d" % h, "sS")
        op("gpsimd", lambda e, h=h: e.dma_start(out=cc_in[h:h + 1, :], in_=crow[0:1, 0:S_OWN]),
           waits=["P1.crow.%d.7" % h], inc="dCC", inc_amt=16)
        op("gpsimd", lambda e, h=h: e.dma_start(out=cc_in[h + 8:h + 9, :], in_=crow[0:1, S_OWN:S_FULL]),
           inc="dCC", inc_amt=16, name="P1.ccw.%d" % h)
        if h + 1 < NHO:
            b1_proj(h + 1, ["P1.cred.%d" % h, "P1.crowdone.%d" % h])

    op("gpsimd", lambda e: e.collective_compute(
        "AllGather", ALU.bypass, replica_groups=[list(range(8))], ins=[cc_in], outs=[cc_out]),
       waits=[("dCC", ctr["dCC"])], inc="sC", name="CC.done")

    barrier("B1")
    esB1b.close()


    esB2v = ExitStack(); esB2v.__enter__()
    v_sb = sbuf(esB2v, "v_sb", [128, 16 * S_OWN], BF16, side="right")
    esB2w = ExitStack(); esB2w.__enter__()
    wvc2 = [sbuf(esB2w, "wvc%d" % i, [128, 2048], BF16, side="right") for i in range(2)]

    for vc in range(16):
        wvc = wvc2[vc % 2]
        op("gpsimd", lambda e, vc=vc, wvc=wvc: e.dma_start(
            out=wvc[:].rearrange("p (ic c) -> p ic c", c=128),
            in_=wv_d[:, vc * 128:(vc + 1) * 128].rearrange("(ic p) c -> p ic c", p=128)),
           waits=[("PV.mmall.%d" % (vc - 2)) if vc >= 2 else None],
           inc="dWV%d" % (vc % 2), inc_amt=16, name="PV.wv.%d" % vc)
        for tch in range(4):
            bank = tch % 2
            if tch >= 2:
                bankdep = "PV.ev.%d.%d" % (vc, tch - 2)
            elif vc >= 1:
                bankdep = "PV.ev.%d.%d" % (vc - 1, tch + 2)
            else:
                bankdep = None
            for ic in range(16):
                op("tensor", lambda e, wvc=wvc, ic=ic, tch=tch, bank=bank: e.matmul(
                    psb[bank], wvc[:, ic * 128:(ic + 1) * 128],
                    xnT_own[:, ic * S_OWN + tch * 512: ic * S_OWN + tch * 512 + 512],
                    start=(ic == 0), stop=(ic == 15)),
                   waits=([("PV.wv.%d" % vc), bankdep] if ic == 0 else []),
                   inc="sT" if ic == 15 else None,
                   name=("PV.mm.%d.%d" % (vc, tch)) if ic == 15 else None)
            op("scalar", lambda e, vc=vc, tch=tch, bank=bank: e.copy(
                out=v_sb[:, vc * S_OWN + tch * 512: vc * S_OWN + tch * 512 + 512],
                in_=psb[bank]),
               waits=["PV.mm.%d.%d" % (vc, tch)], inc="sS", name="PV.ev.%d.%d" % (vc, tch))
        mark("PV.mmall.%d" % vc, "sT")
        drain(pw_thunks, 8)

    barrier("B2")
    esB2w.close()
    esB1a.close()

    esC = ExitStack(); esC.__enter__()
    ccs_sb = sbuf(esC, "ccs_sb", [128, S_OWN], F32)
    c_sb = sbuf(esC, "c_sb", [16, S_OWN], F32)
    c1_2 = [sbuf(esC, "c1_%d" % i, [1, S_OWN], F32) for i in range(2)]
    wo2 = [sbuf(esC, "wo%d" % i, [128, 16 * 512], BF16) for i in range(2)]
    sat2 = [sbuf(esC, "sat%d" % i, [128, 512], BF16) for i in range(2)]

    op("gpsimd", lambda e: e.dma_start(out=ccs_sb[:], in_=cc_out),
       waits=["CC.done"], inc="dCS", inc_amt=16, name="C.ccs")
    for nch in range(4):
        op("tensor", lambda e, nch=nch: e.matmul(
            psB[0:16, 1024:1536], sel_sb[:], ccs_sb[:, nch * 512:(nch + 1) * 512],
            start=True, stop=True),
           waits=["C.ccs", "sel_ld", ("C.cev.%d" % (nch - 1)) if nch >= 1 else None],
           inc="sT", name="C.cmm.%d" % nch)
        op("scalar", lambda e, nch=nch: e.copy(
            out=c_sb[0:16, nch * 512:(nch + 1) * 512], in_=psB[0:16, 1024:1536]),
           waits=["C.cmm.%d" % nch], inc="sS", name="C.cev.%d" % nch)
    mark("C.csel", "sS")
    op("gpsimd", lambda e: e.dma_start(out=c_dram, in_=c_sb[0:16, :]),
       waits=["C.csel"], inc="dCD", inc_amt=16, name="C.cd")

    for hh in range(16):
        c1 = c1_2[hh % 2]
        op("gpsimd", lambda e, hh=hh, c1=c1: e.dma_start(out=c1[0:1, :], in_=c_dram[hh:hh + 1, :]),
           waits=["C.cd", ("C.bc.%d.3" % (hh - 2)) if hh >= 2 else None],
           inc="dC1%d" % (hh % 2), inc_amt=16, name="C.c1.%d" % hh)
        for nch in range(4):
            fl = hh * 4 + nch
            bank = 4 + fl % 2
            if fl >= 2:
                pt, pn = divmod(fl - 2, 4)
                bankdep = "C.vsc.%d.%d" % (pt, pn)
            else:
                bankdep = None
            op("tensor", lambda e, c1=c1, nch=nch, bank=bank: e.matmul(
                psb[bank], ones_r[0:1, :], c1[0:1, nch * 512:(nch + 1) * 512],
                start=True, stop=True),
               waits=["C.c1.%d" % hh, "init_ones", bankdep],
               inc="sT", name="C.bc.%d.%d" % (hh, nch))
            op("vector", lambda e, hh=hh, nch=nch, bank=bank: e.tensor_tensor(
                out=v_sb[:, hh * S_OWN + nch * 512: hh * S_OWN + nch * 512 + 512],
                in0=v_sb[:, hh * S_OWN + nch * 512: hh * S_OWN + nch * 512 + 512],
                in1=psb[bank], op=ALU.mult),
               waits=["C.bc.%d.%d" % (hh, nch)], inc="sV", name="C.vsc.%d.%d" % (hh, nch))
    mark("C.vall", "sV")
    drain(pw_thunks, 30)

    for oc in range(4):
        wo_sb = wo2[oc % 2]
        op("gpsimd", lambda e, oc=oc, wo_sb=wo_sb: e.dma_start(
            out=wo_sb[:].rearrange("p (ic c) -> p ic c", c=512),
            in_=wo_d[:, oc * 512:(oc + 1) * 512].rearrange("(ic p) c -> p ic c", p=128)),
           waits=[("C.smmall.%d" % (oc - 2)) if oc >= 2 else None],
           inc="dWO%d" % (oc % 2), inc_amt=16, name="C.wo.%d" % oc)
        for tc in range(16):
            fl = oc * 16 + tc
            bank = fl % 2
            if fl >= 2:
                po, ptc = divmod(fl - 2, 16)
                bankdep = "C.sev.%d.%d" % (po, ptc)
                satdep = "C.sat.%d.%d" % (po, ptc)
            else:
                bankdep = satdep = None
            for fc in range(16):
                op("tensor", lambda e, wo_sb=wo_sb, tc=tc, fc=fc, bank=bank: e.matmul(
                    psb[bank], v_sb[:, fc * S_OWN + tc * 128:fc * S_OWN + tc * 128 + 128],
                    wo_sb[:, fc * 512:(fc + 1) * 512], start=(fc == 0), stop=(fc == 15)),
                   waits=([("C.wo.%d" % oc), "C.vall", bankdep] if fc == 0 else []),
                   inc="sT" if fc == 15 else None,
                   name=("C.smm.%d.%d" % (oc, tc)) if fc == 15 else None)
            sat = sat2[fl % 2]
            op("scalar", lambda e, sat=sat, bank=bank: e.copy(out=sat[:], in_=psb[bank]),
               waits=["C.smm.%d.%d" % (oc, tc), satdep],
               inc="sS", name="C.sev.%d.%d" % (oc, tc))
            op("gpsimd", lambda e, oc=oc, tc=tc, sat=sat: e.dma_start(
                out=sat_d[tc * 128:(tc + 1) * 128, oc * 512:(oc + 1) * 512], in_=sat[:]),
               waits=["C.sev.%d.%d" % (oc, tc)], inc="dSA%d" % ((oc * 16 + tc) % 2), inc_amt=16,
               name="C.sat.%d.%d" % (oc, tc))
        mark("C.smmall.%d" % oc, "sT")
        drain(pw_thunks, 10)
        drain(tw_thunks, 30)

    drain(pw_thunks, 10000)
    drain(tw_thunks, 10000)
    barrier("C")
    esC.close()
    esB2v.close()
    esPW.close()

    esD = ExitStack(); esD.__enter__()
    qxg2 = [sbuf(esD, "qxg%d" % i, [128, 16 * 256], BF16) for i in range(2)]
    wgt2 = [sbuf(esD, "wgt%d" % i, [128, 16 * 512], BF16) for i in range(2)]
    wut2 = [sbuf(esD, "wut%d" % i, [128, 16 * 512], BF16) for i in range(2)]
    trows = [sbuf(esD, "trow%d" % i, [128, I_FFN], F32) for i in range(2)]
    sg2 = [sbuf(esD, "sg%d" % i, [128, 512], F32) for i in range(4)]
    mcols = sbuf(esD, "mcols", [128, 32], F32)
    q8r = sbuf(esD, "q8r", [128, I_FFN], FP8)
    qtr2 = [sbuf(esD, "qtr%d" % i, [128, I_FFN], BF16) for i in range(2)]
    qsm = sbuf(esD, "qsm", [128, 8], F32)

    NG = 11
    for tcg in range(8):
        qxg = qxg2[tcg % 2]
        for ic in range(16):
            op("scalar", lambda e, tcg=tcg, ic=ic, qxg=qxg: e.dma_start(
                out=qxg[:, ic * 256:(ic + 1) * 256],
                in_=xq_d[tcg * 256:(tcg + 1) * 256, ic * 128:(ic + 1) * 128], transpose=True),
               waits=[("D.mmlast.%d" % (tcg - 2)) if (ic == 0 and tcg >= 2) else None],
               inc="dQX%d" % (tcg % 2), inc_amt=16)
        mark("D.qx.%d" % tcg, "dQX%d" % (tcg % 2))
        for og in range(NG):
            wgt, wut = wgt2[og % 2], wut2[og % 2]
            if og >= 2:
                wdep = "D.wused.%d.%d" % (tcg, og - 2)
            elif tcg >= 1:
                prev_og = NG - 1 - ((NG - 1 - og) % 2)
                wdep = "D.wused.%d.%d" % (tcg - 1, prev_og)
            else:
                wdep = None
            op("sync", lambda e, og=og, wgt=wgt: e.dma_start(
                out=wgt[:].rearrange("p (ic c) -> p ic c", c=512),
                in_=wgT_d[:, og * 512:(og + 1) * 512].rearrange("(ic p) c -> p ic c", p=128)),
               waits=[wdep], inc="dWG%d" % (og % 2), inc_amt=16, name="D.wg.%d.%d" % (tcg, og))
            op("scalar", lambda e, og=og, wut=wut: e.dma_start(
                out=wut[:].rearrange("p (ic c) -> p ic c", c=512),
                in_=wuT_d[:, og * 512:(og + 1) * 512].rearrange("(ic p) c -> p ic c", p=128)),
               waits=[wdep], inc="dWG%d" % (og % 2), inc_amt=16, name="D.wu.%d.%d" % (tcg, og))
            for tci in range(2):
                gbank, ubank = tci + (og % 2) * 4, 2 + tci + (og % 2) * 4
                if og >= 2:
                    gdep = "D.tt.%d.%d.%d" % (tcg, og - 2, tci)
                elif tcg >= 1:
                    prev_og2 = NG - 1 - ((NG - 1 - og) % 2)
                    gdep = "D.tt.%d.%d.%d" % (tcg - 1, prev_og2, tci)
                else:
                    gdep = None
                for ic in range(16):
                    op("tensor", lambda e, qxg=qxg, wgt=wgt, ic=ic, tci=tci, gbank=gbank: e.matmul(
                        psb[gbank], qxg[:, ic * 256 + tci * 128: ic * 256 + tci * 128 + 128],
                        wgt[:, ic * 512:(ic + 1) * 512], start=(ic == 0), stop=(ic == 15)),
                       waits=([("D.qx.%d" % tcg), ("D.wu.%d.%d" % (tcg, og)), gdep]
                              if ic == 0 else []),
                       inc="sT" if ic == 15 else None,
                       name=("D.gmm.%d.%d.%d" % (tcg, og, tci)) if ic == 15 else None)
                    op("tensor", lambda e, qxg=qxg, wut=wut, ic=ic, tci=tci, ubank=ubank: e.matmul(
                        psb[ubank], qxg[:, ic * 256 + tci * 128: ic * 256 + tci * 128 + 128],
                        wut[:, ic * 512:(ic + 1) * 512], start=(ic == 0), stop=(ic == 15)),
                       inc="sT" if ic == 15 else None,
                       name=("D.umm.%d.%d.%d" % (tcg, og, tci)) if ic == 15 else None)
                sg = sg2[tci + (og % 2) * 2]
                op("scalar", lambda e, sg=sg, gbank=gbank: e.activation(
                    out=sg[:], in_=psb[gbank], func=AF.Silu),
                   waits=["D.gmm.%d.%d.%d" % (tcg, og, tci), gdep],
                   inc="sS", name="D.sg.%d.%d.%d" % (tcg, og, tci))
                tr = trows[tci]
                ttw = ["D.umm.%d.%d.%d" % (tcg, og, tci), "D.sg.%d.%d.%d" % (tcg, og, tci)]
                if og == 0 and tcg >= 1:
                    ttw.append("D.qt.%d.%d" % (tcg - 1, tci))
                op("vector", lambda e, tr=tr, sg=sg, og=og, ubank=ubank: e.tensor_tensor(
                    out=tr[:, og * 512:(og + 1) * 512], in0=sg[:], in1=psb[ubank],
                    op=ALU.mult),
                   waits=ttw, inc="sV", name="D.tt.%d.%d.%d" % (tcg, og, tci))
                op("vector", lambda e, tr=tr, og=og, tci=tci: e.reduce_max(
                    mcols[:, tci * 16 + og: tci * 16 + og + 1], tr[:, og * 512:(og + 1) * 512],
                    axis=AX, apply_absolute_value=True),
                   waits=["D.tt.%d.%d.%d" % (tcg, og, tci)], inc="sV",
                   name="D.mx.%d.%d.%d" % (tcg, og, tci))
            mark("D.wused.%d.%d" % (tcg, og), "sT")
        mark("D.mmlast.%d" % tcg, "sT")
        for tci in range(2):
            tr = trows[tci]
            qtrow = qtr2[tci]
            op("vector", lambda e, tci=tci: e.reduce_max(
                qsm[:, tci * 4:tci * 4 + 1], mcols[:, tci * 16:tci * 16 + NG], axis=AX),
               waits=["D.mx.%d.%d.%d" % (tcg, NG - 1, tci)], inc="sV",
               name="D.m.%d.%d" % (tcg, tci))
            op("vector", lambda e, tci=tci: e.tensor_scalar_mul(
                qsm[:, tci * 4 + 1:tci * 4 + 2], qsm[:, tci * 4:tci * 4 + 1], 1.0 / 224.0),
               waits=["D.m.%d.%d" % (tcg, tci)], inc="sV", name="D.msc.%d.%d" % (tcg, tci))
            op("vector", lambda e, tci=tci: e.reciprocal(
                qsm[:, tci * 4 + 2:tci * 4 + 3], qsm[:, tci * 4 + 1:tci * 4 + 2]),
               waits=["D.msc.%d.%d" % (tcg, tci)], inc="sV", name="D.minv.%d.%d" % (tcg, tci))
            op("scalar", lambda e, tr=tr, tci=tci: e.activation(
                out=q8r[:], in_=tr[:], func=AF.Copy, scale=qsm[:, tci * 4 + 2:tci * 4 + 3]),
               waits=["D.minv.%d.%d" % (tcg, tci),
                      ("D.qt.%d.%d" % (tcg, tci - 1)) if tci >= 1 else
                      (("D.qt.%d.1" % (tcg - 1)) if tcg >= 1 else None)],
               inc="sS", name="D.q8.%d.%d" % (tcg, tci))
            op("scalar", lambda e, qtrow=qtrow, tci=tci: e.activation(
                out=qtrow[:], in_=q8r[:], func=AF.Copy, scale=qsm[:, tci * 4 + 1:tci * 4 + 2]),
               waits=["D.q8.%d.%d" % (tcg, tci),
                      ("D.qtw.%d.%d" % (tcg - 1, tci)) if tcg >= 1 else None],
               inc="sS", name="D.qt.%d.%d" % (tcg, tci))
            op("gpsimd", lambda e, tcg=tcg, tci=tci, qtrow=qtrow: e.dma_start(
                out=qt_d[tcg * 256 + tci * 128: tcg * 256 + (tci + 1) * 128, :], in_=qtrow[:]),
               waits=["D.qt.%d.%d" % (tcg, tci)], inc="dQW%d" % tci, inc_amt=16,
               name="D.qtw.%d.%d" % (tcg, tci))

    barrier("D")
    esD.close()

    esE = ExitStack(); esE.__enter__()
    qtt_sb = sbuf(esE, "qtt_sb", [128, 44 * 512], BF16)
    wdg_sb = sbuf(esE, "wdg_sb", [128, 11 * 2048], BF16)
    yac_sb = sbuf(esE, "yac_sb", [128, 4 * 2048], F32)
    gf2 = [sbuf(esE, "gf%d" % i, [128, 512], BF16) for i in range(2)]
    st16 = sbuf(esE, "st16", [128, 4 * 2048], BF16)
    xt16 = sbuf(esE, "xt16", [128, 4 * 2048], F32)
    tv2 = [sbuf(esE, "tv2%d" % i, [128, 512], F32) for i in range(2)]
    ot2 = [sbuf(esE, "ot2%d" % i, [128, 512], F32) for i in range(2)]

    NOC = I_FFN // 128
    for sup in range(4):
        for oc in range(NOC):
            op("scalar", lambda e, sup=sup, oc=oc: e.dma_start(
                out=qtt_sb[:, oc * 512:(oc + 1) * 512],
                in_=qt_d[sup * 512:(sup + 1) * 512, oc * 128:(oc + 1) * 128], transpose=True),
               waits=[("E.mmdone.%d" % (sup - 1)) if (oc == 0 and sup >= 1) else None],
               inc="dQT", inc_amt=16)
        mark("E.qtt.%d" % sup, "dQT")
        for tc in range(4):
            r0 = sup * 512 + tc * 128
            op("sync", lambda e, r0=r0, tc=tc: e.dma_start(
                out=st16[:, tc * 2048:(tc + 1) * 2048], in_=sat_d[r0:r0 + 128, :]),
               waits=[("E.cmb1.%d.3.3" % (sup - 1)) if (tc == 0 and sup >= 1) else None],
               inc="dCB0", inc_amt=16)
            op("sync", lambda e, r0=r0, tc=tc: e.dma_start(
                out=xt16[:, tc * 2048:(tc + 1) * 2048], in_=x_d[r0:r0 + 128, :]),
               inc="dCB1", inc_amt=16)
        cp["E.sx.%d" % sup] = ("dCB1", ctr["dCB1"])
        cp["E.sx0.%d" % sup] = ("dCB0", ctr["dCB0"])
        for ocg in range(4):
            for j in range(11):
                oc = ocg * 11 + j
                if j == 0:
                    if ocg >= 1:
                        wddep = "E.accv.%d.%d" % (sup, ocg - 1)
                    elif sup >= 1:
                        wddep = "E.accv.%d.3" % (sup - 1)
                    else:
                        wddep = None
                else:
                    wddep = None
                op("sync", lambda e, oc=oc, j=j: e.dma_start(
                    out=wdg_sb[:, j * 2048:(j + 1) * 2048], in_=wdT_d[oc * 128:(oc + 1) * 128, :]),
                   waits=[wddep], inc="dWD", inc_amt=16)
            mark("E.wd.%d.%d" % (sup, ocg), "dWD")
            for tc in range(4):
                fi = ocg * 4 + tc
                gfi = sup * 16 + fi
                if gfi >= 2:
                    pfi = fi - 2 if fi >= 2 else fi + 14
                    psup = sup if fi >= 2 else sup - 1
                    bdep = "E.vac.%d.%d.%d" % (psup, pfi // 4, pfi % 4)
                else:
                    bdep = None
                gb = (fi % 2) * 4
                for j in range(11):
                    for og in range(4):
                        op("tensor", lambda e, ocg=ocg, tc=tc, j=j, og=og, gb=gb: e.matmul(
                            psb[gb + og],
                            qtt_sb[:, (ocg * 11 + j) * 512 + tc * 128: (ocg * 11 + j) * 512 + tc * 128 + 128],
                            wdg_sb[:, j * 2048 + og * 512: j * 2048 + og * 512 + 512],
                            start=(j == 0), stop=(j == 10)),
                           waits=([("E.qtt.%d" % sup), ("E.wd.%d.%d" % (sup, ocg)), bdep]
                                  if (j == 0 and og == 0) else []),
                           inc="sT" if j == 10 else None,
                           name=("E.ymm.%d.%d.%d.%d" % (sup, ocg, tc, og)) if j == 10 else None)
                for og in range(4):
                    if ocg == 0:
                        yw = ["E.ymm.%d.%d.%d.%d" % (sup, ocg, tc, og)]
                        if sup >= 1:
                            yw.append("E.gfall.%d" % (sup - 1))
                        op("vector", lambda e, tc=tc, og=og, gb=gb: e.tensor_copy(
                            out=yac_sb[:, tc * 2048 + og * 512: tc * 2048 + og * 512 + 512],
                            in_=psb[gb + og]),
                           waits=yw, inc="sV", name="E.yac.%d.%d.%d.%d" % (sup, ocg, tc, og))
                    else:
                        op("vector", lambda e, tc=tc, og=og, gb=gb: e.tensor_tensor(
                            out=yac_sb[:, tc * 2048 + og * 512: tc * 2048 + og * 512 + 512],
                            in0=yac_sb[:, tc * 2048 + og * 512: tc * 2048 + og * 512 + 512],
                            in1=psb[gb + og], op=ALU.add),
                           waits=["E.ymm.%d.%d.%d.%d" % (sup, ocg, tc, og)],
                           inc="sV", name="E.yac.%d.%d.%d.%d" % (sup, ocg, tc, og))
                mark("E.vac.%d.%d.%d" % (sup, ocg, tc), "sV")
            mark("E.accv.%d.%d" % (sup, ocg), "sV")
        mark("E.mmdone.%d" % sup, "sT")
        for tc in range(4):
            for og in range(4):
                fl = tc * 4 + og
                pi = fl % 2
                gf, tv, ot = gf2[pi], tv2[pi], ot2[pi]
                r0 = sup * 512 + tc * 128
                if fl >= 2:
                    ptc, pog = divmod(fl - 2, 4)
                    bufdep = "E.cmb1.%d.%d.%d" % (sup, ptc, pog)
                elif sup >= 1:
                    bufdep = "E.cmb1.%d.3.%d" % (sup - 1, 2 + og)
                else:
                    bufdep = None
                op("scalar", lambda e, tc=tc, og=og, gf=gf: e.activation(
                    out=gf[:], in_=yac_sb[:, tc * 2048 + og * 512: tc * 2048 + og * 512 + 512],
                    func=AF.Silu),
                   waits=["E.accv.%d.3" % sup, bufdep],
                   inc="sS", name="E.gf.%d.%d.%d" % (sup, tc, og))
                op("vector", lambda e, gf=gf, tv=tv, tc=tc, og=og: e.tensor_tensor(
                    out=tv[:], in0=gf[:], in1=st16[:, tc * 2048 + og * 512: tc * 2048 + og * 512 + 512],
                    op=ALU.mult),
                   waits=["E.sx0.%d" % sup, "E.gf.%d.%d.%d" % (sup, tc, og)],
                   inc="sV", name="E.tv.%d.%d.%d" % (sup, tc, og))
                if fl >= 2:
                    podep = "E.od.%d.%d.%d" % (sup, ptc, pog)
                elif sup >= 1:
                    podep = "E.od.%d.3.%d" % (sup - 1, 2 + og)
                else:
                    podep = None
                op("vector", lambda e, tv=tv, ot=ot, tc=tc, og=og: e.tensor_tensor(
                    out=ot[:], in0=tv[:], in1=xt16[:, tc * 2048 + og * 512: tc * 2048 + og * 512 + 512],
                    op=ALU.add),
                   waits=["E.sx.%d" % sup, "E.tv.%d.%d.%d" % (sup, tc, og), podep],
                   inc="sV", name="E.cmb1.%d.%d.%d" % (sup, tc, og))
                op("sync", lambda e, r0=r0, og=og, ot=ot: e.dma_start(
                    out=out_d[r0:r0 + 128, og * 512:(og + 1) * 512], in_=ot[:]),
                   waits=["E.cmb1.%d.%d.%d" % (sup, tc, og)], inc="dO%d" % (fl % 2), inc_amt=16,
                   name="E.od.%d.%d.%d" % (sup, tc, og))
        mark("E.gfall.%d" % sup, "sS")

    op("sync", lambda e: None, waits=[("dO0", ctr["dO0"]), ("dO1", ctr["dO1"])])
    op("gpsimd", lambda e: None, waits=[("dQW0", ctr["dQW0"]), ("dQW1", ctr["dQW1"])])
    flush()
    esE.close()
    es_top.close()
    return nc


_NC_CACHE = None

def kernel(L0_attn_x, L0_attn_w_qkv, L0_attn_w_o, L0_ffn_w_gate, L0_ffn_w_up,
           L0_ffn_w_down, L0_ffn_up_reorder_index, L0_ffn_down_reorder_index):
    global _NC_CACHE
    x = np.ascontiguousarray(np.asarray(L0_attn_x, dtype=np.float32))
    w_qkv = np.ascontiguousarray(np.asarray(L0_attn_w_qkv, dtype=np.float32))
    w_o = np.ascontiguousarray(np.asarray(L0_attn_w_o, dtype=np.float32))
    w_g = np.ascontiguousarray(np.asarray(L0_ffn_w_gate, dtype=np.float32))
    w_u = np.ascontiguousarray(np.asarray(L0_ffn_w_up, dtype=np.float32))
    w_d = np.asarray(L0_ffn_w_down, dtype=np.float32)
    down_idx = np.asarray(L0_ffn_down_reorder_index)
    wd_perm = np.ascontiguousarray(w_d[:, np.argsort(down_idx)])
    wv = np.ascontiguousarray(w_qkv[:, 4096:6144])

    if _NC_CACHE is None:
        _NC_CACHE = build_nc()
    nc = _NC_CACHE

    in_maps = []
    for c in range(8):
        b, p = c // 2, c % 2
        xb = x[b]
        xrot = np.ascontiguousarray(np.concatenate([xb[S_OWN * p:], xb[:S_OWN * p]], axis=0)) if p else xb
        wqk = np.ascontiguousarray(np.concatenate(
            [w_qkv[:, 1024 * p:1024 * p + 1024],
             w_qkv[:, 2048 + 1024 * p:2048 + 1024 * p + 1024]], axis=1))
        S = np.zeros((128, 16), np.float32)
        for h in range(16):
            r_h = 2 * b + h // 8
            half = 0 if p == (r_h % 2) else 1
            S[16 * r_h + (h % 8) + 8 * half, h] = 1.0
        in_maps.append({
            "x": xrot, "wqk": wqk, "wv": wv, "wo": w_o,
            "wg": w_g, "wu": w_u, "wd": wd_perm, "sel": S,
        })
    res = run_bass_kernel_spmd(nc, in_maps, core_ids=list(range(8)))
    out = np.empty((4, S_FULL, D), np.float32)
    for c in range(8):
        b, p = c // 2, c % 2
        out[b, S_OWN * p:S_OWN * (p + 1)] = res.results[c]["out"]
    return out


# version 18
# speedup vs baseline: 1.1419x; 1.1419x over previous
import numpy as np
from contextlib import ExitStack

import concourse.bass as bass
import concourse.mybir as mybir
from concourse.bass_utils import run_bass_kernel_spmd

F32 = mybir.dt.float32
BF16 = mybir.dt.bfloat16
FP8 = mybir.dt.float8e4
AF = mybir.ActivationFunctionType
ALU = mybir.AluOpType
AX = mybir.AxisListType.X

S_FULL, S_OWN, D = 4096, 2048, 2048
NH, DH, NHO = 16, 128, 8
I_FFN = 5632
CSH = 20.0
ASC = float(1.0 / np.sqrt(np.float64(DH)))
ENGINES = ("sync", "tensor", "scalar", "vector", "gpsimd")


def build_nc():
    nc = bass.Bass("TRN2", target_bir_lowering=False)

    x_d = nc.dram_tensor("x", [S_FULL, D], F32, kind="ExternalInput").ap()
    wqk_d = nc.dram_tensor("wqk", [D, 2048], F32, kind="ExternalInput").ap()
    wv_d = nc.dram_tensor("wv", [D, D], F32, kind="ExternalInput").ap()
    wo_d = nc.dram_tensor("wo", [D, D], F32, kind="ExternalInput").ap()
    wg_d = nc.dram_tensor("wg", [I_FFN, D], F32, kind="ExternalInput").ap()
    wu_d = nc.dram_tensor("wu", [I_FFN, D], F32, kind="ExternalInput").ap()
    wd_d = nc.dram_tensor("wd", [D, I_FFN], F32, kind="ExternalInput").ap()
    sel_d = nc.dram_tensor("sel", [128, 16], F32, kind="ExternalInput").ap()
    out_d = nc.dram_tensor("out", [S_OWN, D], F32, kind="ExternalOutput").ap()

    xn16_d = nc.dram_tensor("xn16_s", [S_FULL, D], BF16).ap()
    xq_d = nc.dram_tensor("xq_s", [S_OWN, D], BF16).ap()
    weffg_d = nc.dram_tensor("weffg_s", [I_FFN, D], BF16).ap()
    weffu_d = nc.dram_tensor("weffu_s", [I_FFN, D], BF16).ap()
    weffd_d = nc.dram_tensor("weffd_s", [D, I_FFN], BF16).ap()
    wgT_d = nc.dram_tensor("wgT_s", [D, I_FFN], BF16).ap()
    wuT_d = nc.dram_tensor("wuT_s", [D, I_FFN], BF16).ap()
    wdT_d = nc.dram_tensor("wdT_s", [I_FFN, D], BF16).ap()
    qt_d = nc.dram_tensor("qt_s", [S_OWN, I_FFN], BF16).ap()
    sat_d = nc.dram_tensor("sat_s", [S_OWN, D], BF16).ap()
    cc_in = nc.dram_tensor("cc_in", [16, S_OWN], F32).ap()
    c_dram = nc.dram_tensor("c_dram", [16, S_OWN], F32).ap()
    cc_out = nc.dram_tensor("cc_out", [128, S_OWN], F32).ap()

    ctr = {}
    cp = {}
    prog = {k: [] for k in ENGINES}

    es_top = ExitStack()
    es_top.__enter__()

    def sbuf(stack, name, shape, dt, side=None):
        return stack.enter_context(nc.sbuf_tensor(name, shape, dt, side=side))

    ones_f = sbuf(es_top, "ones_f", [128, 1], F32)
    cbias = sbuf(es_top, "cbias", [128, 2], F32)
    ones_r = sbuf(es_top, "ones_r", [1, 128], F32)
    sel_sb = sbuf(es_top, "sel_sb", [128, 16], F32)
    c_sb = sbuf(es_top, "c_sb", [16, S_OWN], F32)
    crow = sbuf(es_top, "crow", [1, S_FULL], F32)

    psA = es_top.enter_context(nc.psum_tensor("psA", [128, 2048], F32))
    psB = es_top.enter_context(nc.psum_tensor("psB", [128, 2048], F32))
    psb = [psA[:, i * 512:(i + 1) * 512] for i in range(4)] + \
          [psB[:, i * 512:(i + 1) * 512] for i in range(4)]

    sems = {}
    for s in ("sT", "sS", "sV", "sG", "sC", "bG",
              "dSel", "dX0", "dX1", "dPW0", "dPW1", "dTW0", "dTW1", "dXT",
              "dOT0", "dOT1", "dQX0", "dQX1", "dWG0", "dWG1", "dQT", "dWD",
              "dCB0", "dCB1", "dO0", "dO1",
              "dN0", "dN1", "dXQ0", "dXQ1", "dWS0", "dWS1", "dTS0", "dTS1",
              "dQK0", "dQK1", "dCC", "dWV0", "dWV1", "dCS", "dCD",
              "dC10", "dC11", "dWO0", "dWO1", "dSA0", "dSA1", "dQW0", "dQW1"):
        sems[s] = es_top.enter_context(nc.semaphore(s))

    def op(eng, fn, waits=(), inc=None, inc_amt=1, name=None):
        wl = []
        for w in waits:
            if w is None:
                continue
            if isinstance(w, str):
                wl.append(cp[w])
            else:
                wl.append(w)
        if inc is not None:
            ctr[inc] = ctr.get(inc, 0) + inc_amt
            val = ctr[inc]
        def run(e, fn=fn, wl=tuple(wl), inc=inc, inc_amt=inc_amt):
            for (s, v) in wl:
                if v > 0:
                    e.wait_ge(sems[s], v)
            inst = fn(e)
            if inc is not None:
                if inst is None:
                    inst = e.nop()
                inst.then_inc(sems[inc], inc_amt)
        prog[eng].append(run)
        if name is not None:
            assert inc is not None
            cp[name] = (inc, val)

    def mark(name, sem):
        cp[name] = (sem, ctr.get(sem, 0))

    block = es_top.enter_context(nc.Block())

    def flush():
        for name in ENGINES:
            lst = prog[name]
            if not lst:
                continue
            prog[name] = []
            getattr(block, name)(lambda e, lst=tuple(lst): [f(e) for f in lst])

    def barrier(tag):
        targets = [(s, ctr.get(s, 0)) for s in sems
                   if s[0] == "d" and ctr.get(s, 0) > 0]
        for name in ENGINES:
            op(name, lambda e: None, waits=targets)
        for name in ("tensor", "scalar", "vector", "gpsimd"):
            op(name, lambda e: e.drain(fusable=False))
        for name in ENGINES:
            op(name, lambda e: None, inc="bG")
        arrive = ctr["bG"]
        for name in ENGINES:
            op(name, lambda e: None, waits=[("bG", arrive)])
        flush()

    HW = 2816
    esPW = ExitStack(); esPW.__enter__()
    wqf2 = [sbuf(esPW, "wqf%d" % i, [128, HW], F32) for i in range(2)]
    wq8 = sbuf(esPW, "wq8", [128, HW], FP8)
    wqb2 = [sbuf(esPW, "wqb%d" % i, [128, HW], BF16) for i in range(2)]
    wtb2 = [sbuf(esPW, "wtb%d" % i, [128, HW], BF16) for i in range(2)]
    wsc = sbuf(esPW, "wsc", [128, 8], F32)


    pw_thunks = []
    tw_thunks = []

    def make_qw(tag, src_d, dst_d, nchunks, width, prevtag, nprev):
        for k in range(nchunks):
            def thunk(k=k, tag=tag, src_d=src_d, dst_d=dst_d, width=width,
                      prevtag=prevtag, nprev=nprev):
                wf, wb = wqf2[k % 2], wqb2[k % 2]
                op("sync", lambda e, k=k, wf=wf: e.dma_start(
                    out=wf[:, :width], in_=src_d[k * 128:(k + 1) * 128, :]),
                   waits=[("%s.bf.%d" % (tag, k - 2)) if k >= 2 else
                          (("%s.bf.last" % prevtag) if prevtag else None)],
                   inc="dPW%d" % (k % 2), inc_amt=16, name="%s.ld.%d" % (tag, k))
                op("vector", lambda e, wf=wf: e.reduce_max(
                    wsc[:, 0:1], wf[:, :width], axis=AX, apply_absolute_value=True),
                   waits=["%s.ld.%d" % (tag, k),
                          ("%s.bf.%d" % (tag, k - 1)) if k >= 1 else
                          (("%s.bf.last" % prevtag) if prevtag else None)],
                   inc="sV", name="%s.m.%d" % (tag, k))
                op("vector", lambda e: e.tensor_scalar_mul(wsc[:, 1:2], wsc[:, 0:1], 1.0 / 224.0),
                   waits=["%s.m.%d" % (tag, k)], inc="sV", name="%s.msc.%d" % (tag, k))
                op("vector", lambda e: e.reciprocal(wsc[:, 2:3], wsc[:, 1:2]),
                   waits=["%s.msc.%d" % (tag, k)], inc="sV", name="%s.minv.%d" % (tag, k))
                op("scalar", lambda e, wf=wf: e.activation(
                    out=wq8[:, :width], in_=wf[:, :width], func=AF.Copy, scale=wsc[:, 2:3]),
                   waits=["%s.minv.%d" % (tag, k),
                          ("%s.bf.%d" % (tag, k - 1)) if k >= 1 else
                          (("%s.bf.last" % prevtag) if prevtag else None)],
                   inc="sS", name="%s.q8.%d" % (tag, k))
                op("scalar", lambda e, wb=wb: e.activation(
                    out=wb[:, :width], in_=wq8[:, :width], func=AF.Copy, scale=wsc[:, 1:2]),
                   waits=["%s.q8.%d" % (tag, k),
                          ("%s.st.%d" % (tag, k - 2)) if k >= 2 else
                          (("%s.st.%d" % (prevtag, nprev - 2 + k)) if prevtag else None)],
                   inc="sS", name="%s.bf.%d" % (tag, k))
                op("gpsimd", lambda e, k=k, wb=wb: e.dma_start(
                    out=dst_d[k * 128:(k + 1) * 128, :], in_=wb[:, :width]),
                   waits=["%s.bf.%d" % (tag, k)], inc="dWS%d" % (k % 2), inc_amt=16,
                   name="%s.st.%d" % (tag, k))
                if k == nchunks - 1:
                    cp["%s.bf.last" % tag] = cp["%s.bf.%d" % (tag, k)]
            pw_thunks.append(thunk)

    def make_qw_halves(tag, src_d, dst_d, nchunks, width, prevtag, nprev):
        half = width // 2
        for k in range(nchunks):
            def thunk(k=k, tag=tag, src_d=src_d, dst_d=dst_d, half=half,
                      prevtag=prevtag, nprev=nprev):
                prev_bf = ("%s.bf1.%d" % (tag, k - 1)) if k >= 1 else                           (("%s.bf.last" % prevtag) if prevtag else None)
                for hx in range(2):
                    op("sync", lambda e, k=k, hx=hx: e.dma_start(
                        out=wqf2[hx][:, :half],
                        in_=src_d[k * 128:(k + 1) * 128, hx * half:(hx + 1) * half]),
                       waits=[prev_bf if hx == 0 else None],
                       inc="dPW%d" % hx, inc_amt=16, name="%s.ld%d.%d" % (tag, hx, k))
                    op("vector", lambda e, hx=hx: e.reduce_max(
                        wsc[:, hx * 3:hx * 3 + 1], wqf2[hx][:, :half],
                        axis=AX, apply_absolute_value=True),
                       waits=["%s.ld%d.%d" % (tag, hx, k), prev_bf],
                       inc="sV", name="%s.m%d.%d" % (tag, hx, k))
                op("vector", lambda e: e.tensor_max(wsc[:, 6:7], wsc[:, 0:1], wsc[:, 3:4]),
                   waits=["%s.m0.%d" % (tag, k), "%s.m1.%d" % (tag, k)],
                   inc="sV", name="%s.mm.%d" % (tag, k))
                op("vector", lambda e: e.tensor_scalar_mul(wsc[:, 1:2], wsc[:, 6:7], 1.0 / 224.0),
                   waits=["%s.mm.%d" % (tag, k)], inc="sV", name="%s.msc.%d" % (tag, k))
                op("vector", lambda e: e.reciprocal(wsc[:, 2:3], wsc[:, 1:2]),
                   waits=["%s.msc.%d" % (tag, k)], inc="sV", name="%s.minv.%d" % (tag, k))
                for hx in range(2):
                    op("scalar", lambda e, hx=hx: e.activation(
                        out=wq8[:, :half], in_=wqf2[hx][:, :half], func=AF.Copy,
                        scale=wsc[:, 2:3]),
                       waits=["%s.minv.%d" % (tag, k),
                              ("%s.bf%d.%d" % (tag, 1, k - 1)) if (hx == 0 and k >= 1) else None,
                              ("%s.bf%d.%d" % (tag, 0, k)) if hx == 1 else None],
                       inc="sS", name="%s.q8%d.%d" % (tag, hx, k))
                    op("scalar", lambda e, hx=hx: e.activation(
                        out=wqb2[hx][:, :half], in_=wq8[:, :half], func=AF.Copy,
                        scale=wsc[:, 1:2]),
                       waits=["%s.q8%d.%d" % (tag, hx, k),
                              ("%s.st%d.%d" % (tag, hx, k - 1)) if k >= 1 else
                              (("%s.st.%d" % (prevtag, nprev - 2 + hx)) if prevtag else None)],
                       inc="sS", name="%s.bf%d.%d" % (tag, hx, k))
                    op("gpsimd", lambda e, k=k, hx=hx: e.dma_start(
                        out=dst_d[k * 128:(k + 1) * 128, hx * half:(hx + 1) * half],
                        in_=wqb2[hx][:, :half]),
                       waits=["%s.bf%d.%d" % (tag, hx, k)], inc="dWS%d" % hx, inc_amt=16,
                       name="%s.st%d.%d" % (tag, hx, k))
                if k == nchunks - 1:
                    def fin(e):
                        return None
                    cp["%s.bf.last" % tag] = cp["%s.bf1.%d" % (tag, k)]
                    cp["pw.ad0"] = ("dWS0", ctr.get("dWS0", 0))
                    cp["pw.ad1"] = ("dWS1", ctr.get("dWS1", 0))
            pw_thunks.append(thunk)

    make_qw("pwg", wg_d, weffg_d, I_FFN // 128, D, None, 0)
    make_qw("pwu", wu_d, weffu_d, I_FFN // 128, D, "pwg", 44)
    make_qw_halves("pwd", wd_d, weffd_d, D // 128, I_FFN, "pwu", 44)

    def make_tw(tag, src_d, dst_d, nchunks, width, previdx):
        nh = (width + HW - 1) // HW
        idx = [previdx]
        for k in range(nchunks):
            for hx in range(nh):
                w0 = hx * HW
                w1 = min(width, w0 + HW)
                fi = idx[0]
                idx[0] += 1
                def thunk(k=k, hx=hx, w0=w0, w1=w1, fi=fi, tag=tag,
                          src_d=src_d, dst_d=dst_d):
                    wt = wtb2[fi % 2]
                    op("sync", lambda e: e.dma_start(
                        out=wt[:, :w1 - w0],
                        in_=src_d[w0:w1, k * 128:(k + 1) * 128], transpose=True),
                       waits=["pw.ad0", "pw.ad1",
                              ("tw.st.%d" % (fi - 2)) if fi - 2 >= 0 else None],
                       inc="dTW%d" % (fi % 2), inc_amt=16, name="tw.ld.%d" % fi)
                    op("gpsimd", lambda e: e.dma_start(
                        out=dst_d[k * 128:(k + 1) * 128, w0:w1], in_=wt[:, :w1 - w0]),
                       waits=["tw.ld.%d" % fi], inc="dTS%d" % (fi % 2), inc_amt=16,
                       name="tw.st.%d" % fi)
                tw_thunks.append(thunk)
        return idx[0]

    _ti = make_tw("twg", weffg_d, wgT_d, D // 128, I_FFN, 0)
    _ti = make_tw("twu", weffu_d, wuT_d, D // 128, I_FFN, _ti)
    _ti = make_tw("twd", weffd_d, wdT_d, I_FFN // 128, D, _ti)

    def drain(lst, n):
        while n > 0 and lst:
            lst.pop(0)()
            n -= 1

    esA = ExitStack(); esA.__enter__()
    xt2 = [sbuf(esA, "xt%d" % i, [128, D], F32) for i in range(2)]
    cent2 = [sbuf(esA, "cent%d" % i, [128, D], F32) for i in range(2)]
    sqt = sbuf(esA, "sqt", [128, D], F32)
    xnf2 = [sbuf(esA, "xnf%d" % i, [128, D], F32) for i in range(2)]
    xn16_2 = [sbuf(esA, "xn16_%d" % i, [128, D], BF16) for i in range(2)]
    q8t2 = [sbuf(esA, "q8t%d" % i, [128, D], FP8) for i in range(2)]
    xeff2 = [sbuf(esA, "xeff%d" % i, [128, D], BF16) for i in range(2)]
    lnsm = sbuf(esA, "lnsm", [128, 16], F32)

    op("vector", lambda e: e.memset(ones_f[:], 1.0), inc="sV")
    op("vector", lambda e: e.memset(cbias[:, 0:1], 1e-5), inc="sV")
    op("vector", lambda e: e.memset(cbias[:, 1:2], -CSH), inc="sV")
    op("vector", lambda e: e.memset(ones_r[:], 1.0), inc="sV", name="init_ones")
    op("sync", lambda e: e.dma_start(out=sel_sb[:], in_=sel_d), inc="dSel", inc_amt=16, name="sel_ld")

    NT = S_FULL // 128
    NOWN = S_OWN // 128
    for it in range(NT):
        pi = it % 2
        own = it < NOWN
        xt, cent, xnf = xt2[pi], cent2[pi], xnf2[pi]
        xn16t, q8t, xeff = xn16_2[pi], q8t2[pi], xeff2[pi]
        def L(k, pi=pi):
            return lnsm[:, 8 * pi + k: 8 * pi + k + 1]
        op("sync", lambda e, it=it, xt=xt: e.dma_start(out=xt[:], in_=x_d[it * 128:(it + 1) * 128, :]),
           waits=[("P0.lastread.%d" % (it - 2)) if it >= 2 else None],
           inc="dX%d" % (it % 2), inc_amt=16, name="P0.xd.%d" % it)
        op("vector", lambda e, xt=xt, L=L: e.reduce_sum(L(0), xt[:], axis=AX),
           waits=["P0.xd.%d" % it, ("P0.lastread.%d" % (it - 2)) if it >= 2 else None],
           inc="sV", name="P0.su.%d" % it)
        op("vector", lambda e, L=L: e.tensor_scalar_mul(L(1), L(0), 1.0 / D),
           waits=["P0.su.%d" % it], inc="sV", name="P0.mu.%d" % it)
        op("vector", lambda e, xt=xt, cent=cent, L=L: e.tensor_scalar(
            out=cent[:], in0=xt[:], scalar1=L(1), scalar2=None, op0=ALU.subtract),
           waits=["P0.mu.%d" % it], inc="sV", name="P0.cent.%d" % it)
        op("scalar", lambda e, cent=cent, L=L: e.activation(
            out=sqt[:], in_=cent[:], func=AF.Square, accum_out=L(2)),
           waits=["P0.cent.%d" % it,
                  ("P0.std.%d" % (it - 2)) if it >= 2 else None],
           inc="sS", name="P0.sq.%d" % it)
        op("scalar", lambda e, L=L: e.activation(
            out=L(4), in_=L(2), func=AF.Sqrt, scale=1.0 / D, bias=cbias[:, 0:1]),
           waits=["P0.sq.%d" % it], inc="sS", name="P0.std.%d" % it)
        op("vector", lambda e, L=L: e.reciprocal(L(5), L(4)),
           waits=["P0.std.%d" % it], inc="sV", name="P0.rstd.%d" % it)
        op("scalar", lambda e, cent=cent, xn16t=xn16t, L=L: e.activation(
            out=xn16t[:], in_=cent[:], func=AF.Copy, scale=L(5)),
           waits=["P0.rstd.%d" % it, ("P0.xnd.%d" % (it - 2)) if it >= 2 else None],
           inc="sS", name="P0.xn16.%d" % it)
        if own:
            op("scalar", lambda e, cent=cent, xnf=xnf, L=L: e.activation(
                out=xnf[:], in_=cent[:], func=AF.Copy, scale=L(5)),
               inc="sS", name="P0.lastread.%d" % it)
        else:
            cp["P0.lastread.%d" % it] = cp["P0.xn16.%d" % it]
        op("gpsimd", lambda e, it=it, xn16t=xn16t: e.dma_start(
            out=xn16_d[it * 128:(it + 1) * 128, :], in_=xn16t[:]),
           waits=["P0.xn16.%d" % it], inc="dN%d" % (it % 2), inc_amt=16, name="P0.xnd.%d" % it)
        if own:
            op("vector", lambda e, xnf=xnf, L=L: e.reduce_max(
                L(6), xnf[:], axis=AX, apply_absolute_value=True),
               waits=["P0.lastread.%d" % it], inc="sV", name="P0.m.%d" % it)
            op("vector", lambda e, L=L: e.tensor_scalar_mul(L(7), L(6), 1.0 / 224.0),
               waits=["P0.m.%d" % it], inc="sV", name="P0.msc.%d" % it)
            op("vector", lambda e, L=L: e.reciprocal(L(3), L(7)),
               waits=["P0.msc.%d" % it], inc="sV", name="P0.minv.%d" % it)
            op("scalar", lambda e, xnf=xnf, q8t=q8t, L=L: e.activation(
                out=q8t[:], in_=xnf[:], func=AF.Copy, scale=L(3)),
               waits=["P0.minv.%d" % it,
                      ("P0.xe.%d" % (it - 2)) if it >= 2 else None],
               inc="sS", name="P0.q8.%d" % it)
            op("scalar", lambda e, q8t=q8t, xeff=xeff, L=L: e.activation(
                out=xeff[:], in_=q8t[:], func=AF.Copy, scale=L(7)),
               waits=["P0.q8.%d" % it, ("P0.xqd.%d" % (it - 2)) if it >= 2 else None],
               inc="sS", name="P0.xe.%d" % it)
            op("gpsimd", lambda e, it=it, xeff=xeff: e.dma_start(
                out=xq_d[it * 128:(it + 1) * 128, :], in_=xeff[:]),
               waits=["P0.xe.%d" % it], inc="dXQ%d" % (it % 2), inc_amt=16, name="P0.xqd.%d" % it)
        if it >= 2:
            drain(pw_thunks, 2)

    barrier("A")
    esA.close()

    esB1a = ExitStack(); esB1a.__enter__()
    xnT_own = sbuf(esB1a, "xnT_own", [128, 16 * S_OWN], BF16)
    esB1b = ExitStack(); esB1b.__enter__()
    qh_sb = sbuf(esB1b, "qh_sb", [128, S_FULL], BF16, side="right")
    kh_sb = sbuf(esB1b, "kh_sb", [128, S_FULL], BF16, side="right")
    e_row2 = [sbuf(esB1b, "e_row%d" % i, [128, S_FULL], BF16, side="right") for i in range(2)]
    c_acc = sbuf(esB1b, "c_acc", [128, S_FULL], F32, side="right")
    wqh2 = [sbuf(esB1b, "wqh%d" % i, [128, 2048], BF16, side="right") for i in range(2)]
    wkh2 = [sbuf(esB1b, "wkh%d" % i, [128, 2048], BF16, side="right") for i in range(2)]
    oth2 = [sbuf(esB1b, "oth%d" % i, [128, S_OWN], BF16, side="right") for i in range(2)]
    zrow = sbuf(esB1b, "zrow", [128, 8], F32, side="right")
    zsm = sbuf(esB1b, "zsm", [128, 8], F32, side="right")

    for ic in range(16):
        op("sync", lambda e, ic=ic: e.dma_start(
            out=xnT_own[:, ic * S_OWN:(ic + 1) * S_OWN],
            in_=xn16_d[0:S_OWN, ic * 128:(ic + 1) * 128], transpose=True),
           inc="dXT", inc_amt=16)
    mark("B1.xnT", "dXT")

    for h in range(NHO):
        hp = h % 2
        wqh, wkh = wqh2[hp], wkh2[hp]
        prev_scores_done = ("P1.exdone.%d" % (h - 1)) if h >= 1 else None
        prev_cred_done = ("P1.cred.%d" % (h - 1)) if h >= 1 else None
        op("gpsimd", lambda e, h=h, wqh=wqh: e.dma_start(
            out=wqh[:].rearrange("p (ic c) -> p ic c", c=128),
            in_=wqk_d[:, h * 128:(h + 1) * 128].rearrange("(ic p) c -> p ic c", p=128)),
           waits=[("P1.qkdone.%d" % (h - 2)) if h >= 2 else None],
           inc="dQK%d" % (h % 2), inc_amt=16, name="P1.wq.%d" % h)
        op("gpsimd", lambda e, h=h, wkh=wkh: e.dma_start(
            out=wkh[:].rearrange("p (ic c) -> p ic c", c=128),
            in_=wqk_d[:, 1024 + h * 128:1024 + (h + 1) * 128].rearrange("(ic p) c -> p ic c", p=128)),
           inc="dQK%d" % (h % 2), inc_amt=16, name="P1.wk.%d" % h)

        for part, w_sb, dst in (("q", wqh, qh_sb), ("k", wkh, kh_sb)):
            for tch in range(4):
                bank = (0 if part == "q" else 2) + tch % 2
                if tch >= 2:
                    bankdep = "P1.ev_%s.%d.%d" % (part, h, tch - 2)
                elif h >= 1:
                    bankdep = "P1.ev_%s.%d.%d" % (part, h - 1, tch + 2)
                else:
                    bankdep = None
                for ic in range(16):
                    op("tensor", lambda e, w_sb=w_sb, ic=ic, tch=tch, bank=bank: e.matmul(
                        psb[bank], w_sb[:, ic * 128:(ic + 1) * 128],
                        xnT_own[:, ic * S_OWN + tch * 512: ic * S_OWN + tch * 512 + 512],
                        start=(ic == 0), stop=(ic == 15)),
                       waits=([("P1.wk.%d" % h), "B1.xnT",
                               prev_scores_done, prev_cred_done, bankdep]
                              if ic == 0 else []),
                       inc="sT" if ic == 15 else None,
                       name=("P1.mm_%s.%d.%d" % (part, h, tch)) if ic == 15 else None)
                op("scalar", lambda e, dst=dst, tch=tch, bank=bank: e.copy(
                    out=dst[:, tch * 512:tch * 512 + 512], in_=psb[bank]),
                   waits=["P1.mm_%s.%d.%d" % (part, h, tch)],
                   inc="sS", name="P1.ev_%s.%d.%d" % (part, h, tch))
        for ic in range(16):
            oth = oth2[ic % 2]
            op("scalar", lambda e, h=h, ic=ic, oth=oth: e.dma_start(
                out=oth[:], in_=xn16_d[S_OWN:S_FULL, ic * 128:(ic + 1) * 128], transpose=True),
               waits=[("P1.othmm.%d.%d" % (h, ic - 2)) if ic >= 2 else
                      (("P1.oevdone.%d" % (h - 1)) if h >= 1 else None)],
               inc="dOT%d" % (ic % 2), inc_amt=16, name="P1.oth.%d.%d" % (h, ic))
            first_waits = ["P1.oth.%d.%d" % (h, ic)]
            if ic == 0:
                first_waits += ["P1.ev_q.%d.3" % h, "P1.ev_k.%d.3" % h]
                if h >= 1:
                    first_waits += ["P1.oevdone.%d" % (h - 1), "P1.crowdone.%d" % (h - 1)]
            nmm = 0
            for part, w_sb in (("q", wqh), ("k", wkh)):
                for tch in range(4):
                    bank = (0 if part == "q" else 4) + tch
                    nmm += 1
                    last = (nmm == 8)
                    op("tensor", lambda e, w_sb=w_sb, oth=oth, ic=ic, tch=tch, bank=bank: e.matmul(
                        psb[bank], w_sb[:, ic * 128:(ic + 1) * 128],
                        oth[:, tch * 512:tch * 512 + 512],
                        start=(ic == 0), stop=(ic == 15)),
                       waits=first_waits if (part == "q" and tch == 0) else [],
                       inc="sT" if last else None,
                       name=("P1.othmm.%d.%d" % (h, ic)) if last else None)
        for part, dst in (("q", qh_sb), ("k", kh_sb)):
            for tch in range(4):
                bank = (0 if part == "q" else 4) + tch
                op("scalar", lambda e, dst=dst, tch=tch, bank=bank: e.copy(
                    out=dst[:, 2048 + tch * 512: 2048 + tch * 512 + 512], in_=psb[bank]),
                   waits=["P1.othmm.%d.15" % h], inc="sS",
                   name="P1.oev_%s.%d.%d" % (part, h, tch))
        mark("P1.oevdone.%d" % h, "sS")
        mark("P1.qkdone.%d" % h, "sS")

        for qt in range(32):
            qp = qt % 2
            e_row = e_row2[qp]
            zc = zrow[:, qp * 4: qp * 4 + 2]
            zs0 = zsm[:, qp * 4: qp * 4 + 1]
            zs1 = zsm[:, qp * 4 + 1: qp * 4 + 2]
            for kc in range(8):
                bank = kc
                mmw = []
                if qt == 0 and kc == 0:
                    mmw.append("P1.qkdone.%d" % h)
                if qt >= 1 and kc % 4 == 0:
                    mmw.append("P1.ex%s.%d.%d" % ("A" if kc < 4 else "B", h, qt - 1))
                op("tensor", lambda e, qt=qt, kc=kc, bank=bank: e.matmul(
                    psb[bank], qh_sb[:, qt * 128:(qt + 1) * 128],
                    kh_sb[:, kc * 512:kc * 512 + 512], start=True, stop=True),
                   waits=mmw, inc="sT" if kc % 4 == 3 else None,
                   name=("P1.sc%s.%d.%d" % ("A" if kc < 4 else "B", h, qt)) if kc % 4 == 3 else None)
                if kc == 3:
                    exw = ["P1.scA.%d.%d" % (h, qt)]
                    if qt >= 2:
                        exw.append("P1.cacc.%d.%d" % (h, qt - 2))
                        exw.append("P1.z.%d.%d" % (h, qt - 2))
                    elif h >= 1:
                        exw.append("P1.cacc.%d.31" % (h - 1))
                        exw.append("P1.z.%d.31" % (h - 1))
                    op("scalar", lambda e, e_row=e_row, zc=zc: e.activation(
                        out=e_row[:, 0:2048], in_=psA[:, :], func=AF.Exp,
                        scale=ASC, bias=cbias[:, 1:2], accum_out=zc[:, 0:1]),
                       waits=exw, inc="sS", name="P1.exA.%d.%d" % (h, qt))
                if kc == 7:
                    op("scalar", lambda e, e_row=e_row, zc=zc: e.activation(
                        out=e_row[:, 2048:4096], in_=psB[:, :], func=AF.Exp,
                        scale=ASC, bias=cbias[:, 1:2], accum_out=zc[:, 1:2]),
                       waits=["P1.scB.%d.%d" % (h, qt)], inc="sS",
                       name="P1.exB.%d.%d" % (h, qt))
            op("vector", lambda e, zc=zc, zs0=zs0: e.reduce_sum(zs0, zc, axis=AX),
               waits=["P1.exB.%d.%d" % (h, qt)], inc="sV", name="P1.zs.%d.%d" % (h, qt))
            op("vector", lambda e, zs0=zs0, zs1=zs1: e.reciprocal(zs1, zs0),
               waits=["P1.zs.%d.%d" % (h, qt)], inc="sV", name="P1.z.%d.%d" % (h, qt))
            if qt == 0:
                op("vector", lambda e, e_row=e_row, zs1=zs1: e.tensor_scalar(
                    out=c_acc[:], in0=e_row[:], scalar1=zs1, scalar2=None, op0=ALU.mult),
                   waits=["P1.z.%d.0" % h, prev_cred_done], inc="sV", name="P1.cacc.%d.0" % h)
            else:
                op("vector", lambda e, e_row=e_row, zs1=zs1: e.scalar_tensor_tensor(
                    out=c_acc[:], in0=e_row[:], scalar=zs1, in1=c_acc[:],
                    op0=ALU.mult, op1=ALU.add),
                   waits=["P1.z.%d.%d" % (h, qt)], inc="sV", name="P1.cacc.%d.%d" % (h, qt))
        for kc in range(8):
            op("tensor", lambda e, kc=kc: e.matmul(
                psB[0:1, 0:512], ones_f[:], c_acc[:, kc * 512:kc * 512 + 512],
                start=True, stop=True),
               waits=["P1.cacc.%d.31" % h, "init_ones",
                      ("P1.crow.%d.%d" % (h, kc - 1)) if kc >= 1 else None],
               inc="sT", name="P1.cr.%d.%d" % (h, kc))
            op("scalar", lambda e, kc=kc: e.copy(
                out=crow[0:1, kc * 512:kc * 512 + 512], in_=psB[0:1, 0:512]),
               waits=["P1.cr.%d.%d" % (h, kc),
                      ("P1.ccw.%d" % (h - 1)) if (kc == 0 and h >= 1) else None],
               inc="sS", name="P1.crow.%d.%d" % (h, kc))
        mark("P1.cred.%d" % h, "sT")
        mark("P1.crowdone.%d" % h, "sS")
        mark("P1.exdone.%d" % h, "sS")
        op("gpsimd", lambda e, h=h: e.dma_start(out=cc_in[h:h + 1, :], in_=crow[0:1, 0:S_OWN]),
           waits=["P1.crow.%d.7" % h], inc="dCC", inc_amt=16)
        op("gpsimd", lambda e, h=h: e.dma_start(out=cc_in[h + 8:h + 9, :], in_=crow[0:1, S_OWN:S_FULL]),
           inc="dCC", inc_amt=16, name="P1.ccw.%d" % h)
        drain(pw_thunks, 8)

    drain(pw_thunks, 1000)
    op("gpsimd", lambda e: e.collective_compute(
        "AllGather", ALU.bypass, replica_groups=[list(range(8))], ins=[cc_in], outs=[cc_out]),
       waits=[("dCC", ctr["dCC"])], inc="sC", name="CC.done")

    barrier("B1")
    esB1b.close()

    esB2v = ExitStack(); esB2v.__enter__()
    v_sb = sbuf(esB2v, "v_sb", [128, 16 * S_OWN], BF16, side="right")
    esB2w = ExitStack(); esB2w.__enter__()
    wvc2 = [sbuf(esB2w, "wvc%d" % i, [128, 2048], BF16, side="right") for i in range(2)]

    for vc in range(16):
        wvc = wvc2[vc % 2]
        op("gpsimd", lambda e, vc=vc, wvc=wvc: e.dma_start(
            out=wvc[:].rearrange("p (ic c) -> p ic c", c=128),
            in_=wv_d[:, vc * 128:(vc + 1) * 128].rearrange("(ic p) c -> p ic c", p=128)),
           waits=[("PV.mmall.%d" % (vc - 2)) if vc >= 2 else None],
           inc="dWV%d" % (vc % 2), inc_amt=16, name="PV.wv.%d" % vc)
        for tch in range(4):
            bank = tch % 2
            if tch >= 2:
                bankdep = "PV.ev.%d.%d" % (vc, tch - 2)
            elif vc >= 1:
                bankdep = "PV.ev.%d.%d" % (vc - 1, tch + 2)
            else:
                bankdep = None
            for ic in range(16):
                op("tensor", lambda e, wvc=wvc, ic=ic, tch=tch, bank=bank: e.matmul(
                    psb[bank], wvc[:, ic * 128:(ic + 1) * 128],
                    xnT_own[:, ic * S_OWN + tch * 512: ic * S_OWN + tch * 512 + 512],
                    start=(ic == 0), stop=(ic == 15)),
                   waits=([("PV.wv.%d" % vc), bankdep] if ic == 0 else []),
                   inc="sT" if ic == 15 else None,
                   name=("PV.mm.%d.%d" % (vc, tch)) if ic == 15 else None)
            op("scalar", lambda e, vc=vc, tch=tch, bank=bank: e.copy(
                out=v_sb[:, vc * S_OWN + tch * 512: vc * S_OWN + tch * 512 + 512],
                in_=psb[bank]),
               waits=["PV.mm.%d.%d" % (vc, tch)], inc="sS", name="PV.ev.%d.%d" % (vc, tch))
        mark("PV.mmall.%d" % vc, "sT")
        drain(tw_thunks, 5)

    barrier("B2")
    esB2w.close()
    esB1a.close()

    esC = ExitStack(); esC.__enter__()
    ccs_sb = sbuf(esC, "ccs_sb", [128, S_OWN], F32)
    c1_2 = [sbuf(esC, "c1_%d" % i, [1, S_OWN], F32) for i in range(2)]
    wo2 = [sbuf(esC, "wo%d" % i, [128, 16 * 512], BF16) for i in range(2)]
    sat2 = [sbuf(esC, "sat%d" % i, [128, 512], BF16) for i in range(2)]

    op("gpsimd", lambda e: e.dma_start(out=ccs_sb[:], in_=cc_out),
       waits=["CC.done"], inc="dCS", inc_amt=16, name="C.ccs")
    for nch in range(4):
        op("tensor", lambda e, nch=nch: e.matmul(
            psB[0:16, 1024:1536], sel_sb[:], ccs_sb[:, nch * 512:(nch + 1) * 512],
            start=True, stop=True),
           waits=["C.ccs", "sel_ld", ("C.cev.%d" % (nch - 1)) if nch >= 1 else None],
           inc="sT", name="C.cmm.%d" % nch)
        op("scalar", lambda e, nch=nch: e.copy(
            out=c_sb[0:16, nch * 512:(nch + 1) * 512], in_=psB[0:16, 1024:1536]),
           waits=["C.cmm.%d" % nch], inc="sS", name="C.cev.%d" % nch)
    mark("C.csel", "sS")
    op("gpsimd", lambda e: e.dma_start(out=c_dram, in_=c_sb[0:16, :]),
       waits=["C.csel"], inc="dCD", inc_amt=16, name="C.cd")

    for hh in range(16):
        c1 = c1_2[hh % 2]
        op("gpsimd", lambda e, hh=hh, c1=c1: e.dma_start(out=c1[0:1, :], in_=c_dram[hh:hh + 1, :]),
           waits=["C.cd", ("C.bc.%d.3" % (hh - 2)) if hh >= 2 else None],
           inc="dC1%d" % (hh % 2), inc_amt=16, name="C.c1.%d" % hh)
        for nch in range(4):
            fl = hh * 4 + nch
            bank = 4 + fl % 2
            if fl >= 2:
                pt, pn = divmod(fl - 2, 4)
                bankdep = "C.vsc.%d.%d" % (pt, pn)
            else:
                bankdep = None
            op("tensor", lambda e, c1=c1, nch=nch, bank=bank: e.matmul(
                psb[bank], ones_r[0:1, :], c1[0:1, nch * 512:(nch + 1) * 512],
                start=True, stop=True),
               waits=["C.c1.%d" % hh, "init_ones", bankdep],
               inc="sT", name="C.bc.%d.%d" % (hh, nch))
            op("vector", lambda e, hh=hh, nch=nch, bank=bank: e.tensor_tensor(
                out=v_sb[:, hh * S_OWN + nch * 512: hh * S_OWN + nch * 512 + 512],
                in0=v_sb[:, hh * S_OWN + nch * 512: hh * S_OWN + nch * 512 + 512],
                in1=psb[bank], op=ALU.mult),
               waits=["C.bc.%d.%d" % (hh, nch)], inc="sV", name="C.vsc.%d.%d" % (hh, nch))
    mark("C.vall", "sV")

    for oc in range(4):
        wo_sb = wo2[oc % 2]
        op("gpsimd", lambda e, oc=oc, wo_sb=wo_sb: e.dma_start(
            out=wo_sb[:].rearrange("p (ic c) -> p ic c", c=512),
            in_=wo_d[:, oc * 512:(oc + 1) * 512].rearrange("(ic p) c -> p ic c", p=128)),
           waits=[("C.smmall.%d" % (oc - 2)) if oc >= 2 else None],
           inc="dWO%d" % (oc % 2), inc_amt=16, name="C.wo.%d" % oc)
        for tc in range(16):
            fl = oc * 16 + tc
            bank = fl % 2
            if fl >= 2:
                po, ptc = divmod(fl - 2, 16)
                bankdep = "C.sev.%d.%d" % (po, ptc)
                satdep = "C.sat.%d.%d" % (po, ptc)
            else:
                bankdep = satdep = None
            for fc in range(16):
                op("tensor", lambda e, wo_sb=wo_sb, tc=tc, fc=fc, bank=bank: e.matmul(
                    psb[bank], v_sb[:, fc * S_OWN + tc * 128:fc * S_OWN + tc * 128 + 128],
                    wo_sb[:, fc * 512:(fc + 1) * 512], start=(fc == 0), stop=(fc == 15)),
                   waits=([("C.wo.%d" % oc), "C.vall", bankdep] if fc == 0 else []),
                   inc="sT" if fc == 15 else None,
                   name=("C.smm.%d.%d" % (oc, tc)) if fc == 15 else None)
            sat = sat2[fl % 2]
            op("scalar", lambda e, sat=sat, bank=bank: e.copy(out=sat[:], in_=psb[bank]),
               waits=["C.smm.%d.%d" % (oc, tc), satdep],
               inc="sS", name="C.sev.%d.%d" % (oc, tc))
            op("gpsimd", lambda e, oc=oc, tc=tc, sat=sat: e.dma_start(
                out=sat_d[tc * 128:(tc + 1) * 128, oc * 512:(oc + 1) * 512], in_=sat[:]),
               waits=["C.sev.%d.%d" % (oc, tc)], inc="dSA%d" % ((oc * 16 + tc) % 2), inc_amt=16,
               name="C.sat.%d.%d" % (oc, tc))
        mark("C.smmall.%d" % oc, "sT")
        drain(tw_thunks, 10)

    drain(tw_thunks, 10000)
    barrier("C")
    esC.close()
    esB2v.close()
    esPW.close()

    esD = ExitStack(); esD.__enter__()
    qxg2 = [sbuf(esD, "qxg%d" % i, [128, 16 * 256], BF16) for i in range(2)]
    wgt2 = [sbuf(esD, "wgt%d" % i, [128, 16 * 512], BF16) for i in range(2)]
    wut2 = [sbuf(esD, "wut%d" % i, [128, 16 * 512], BF16) for i in range(2)]
    trows = [sbuf(esD, "trow%d" % i, [128, I_FFN], F32) for i in range(2)]
    sg2 = [sbuf(esD, "sg%d" % i, [128, 512], F32) for i in range(4)]
    mcols = sbuf(esD, "mcols", [128, 32], F32)
    q8r = sbuf(esD, "q8r", [128, I_FFN], FP8)
    qtr2 = [sbuf(esD, "qtr%d" % i, [128, I_FFN], BF16) for i in range(2)]
    qsm = sbuf(esD, "qsm", [128, 8], F32)

    NG = 11
    for tcg in range(8):
        qxg = qxg2[tcg % 2]
        for ic in range(16):
            op("scalar", lambda e, tcg=tcg, ic=ic, qxg=qxg: e.dma_start(
                out=qxg[:, ic * 256:(ic + 1) * 256],
                in_=xq_d[tcg * 256:(tcg + 1) * 256, ic * 128:(ic + 1) * 128], transpose=True),
               waits=[("D.mmlast.%d" % (tcg - 2)) if (ic == 0 and tcg >= 2) else None],
               inc="dQX%d" % (tcg % 2), inc_amt=16)
        mark("D.qx.%d" % tcg, "dQX%d" % (tcg % 2))
        for og in range(NG):
            wgt, wut = wgt2[og % 2], wut2[og % 2]
            if og >= 2:
                wdep = "D.wused.%d.%d" % (tcg, og - 2)
            elif tcg >= 1:
                prev_og = NG - 1 - ((NG - 1 - og) % 2)
                wdep = "D.wused.%d.%d" % (tcg - 1, prev_og)
            else:
                wdep = None
            op("sync", lambda e, og=og, wgt=wgt: e.dma_start(
                out=wgt[:].rearrange("p (ic c) -> p ic c", c=512),
                in_=wgT_d[:, og * 512:(og + 1) * 512].rearrange("(ic p) c -> p ic c", p=128)),
               waits=[wdep], inc="dWG%d" % (og % 2), inc_amt=16, name="D.wg.%d.%d" % (tcg, og))
            op("sync", lambda e, og=og, wut=wut: e.dma_start(
                out=wut[:].rearrange("p (ic c) -> p ic c", c=512),
                in_=wuT_d[:, og * 512:(og + 1) * 512].rearrange("(ic p) c -> p ic c", p=128)),
               inc="dWG%d" % (og % 2), inc_amt=16, name="D.wu.%d.%d" % (tcg, og))
            for tci in range(2):
                gbank, ubank = tci + (og % 2) * 4, 2 + tci + (og % 2) * 4
                if og >= 2:
                    gdep = "D.tt.%d.%d.%d" % (tcg, og - 2, tci)
                elif tcg >= 1:
                    prev_og2 = NG - 1 - ((NG - 1 - og) % 2)
                    gdep = "D.tt.%d.%d.%d" % (tcg - 1, prev_og2, tci)
                else:
                    gdep = None
                for ic in range(16):
                    op("tensor", lambda e, qxg=qxg, wgt=wgt, ic=ic, tci=tci, gbank=gbank: e.matmul(
                        psb[gbank], qxg[:, ic * 256 + tci * 128: ic * 256 + tci * 128 + 128],
                        wgt[:, ic * 512:(ic + 1) * 512], start=(ic == 0), stop=(ic == 15)),
                       waits=([("D.qx.%d" % tcg), ("D.wu.%d.%d" % (tcg, og)), gdep]
                              if ic == 0 else []),
                       inc="sT" if ic == 15 else None,
                       name=("D.gmm.%d.%d.%d" % (tcg, og, tci)) if ic == 15 else None)
                    op("tensor", lambda e, qxg=qxg, wut=wut, ic=ic, tci=tci, ubank=ubank: e.matmul(
                        psb[ubank], qxg[:, ic * 256 + tci * 128: ic * 256 + tci * 128 + 128],
                        wut[:, ic * 512:(ic + 1) * 512], start=(ic == 0), stop=(ic == 15)),
                       inc="sT" if ic == 15 else None,
                       name=("D.umm.%d.%d.%d" % (tcg, og, tci)) if ic == 15 else None)
                sg = sg2[tci + (og % 2) * 2]
                op("scalar", lambda e, sg=sg, gbank=gbank: e.activation(
                    out=sg[:], in_=psb[gbank], func=AF.Silu),
                   waits=["D.gmm.%d.%d.%d" % (tcg, og, tci), gdep],
                   inc="sS", name="D.sg.%d.%d.%d" % (tcg, og, tci))
                tr = trows[tci]
                ttw = ["D.umm.%d.%d.%d" % (tcg, og, tci), "D.sg.%d.%d.%d" % (tcg, og, tci)]
                if og == 0 and tcg >= 1:
                    ttw.append("D.qt.%d.%d" % (tcg - 1, tci))
                op("vector", lambda e, tr=tr, sg=sg, og=og, ubank=ubank: e.tensor_tensor(
                    out=tr[:, og * 512:(og + 1) * 512], in0=sg[:], in1=psb[ubank],
                    op=ALU.mult),
                   waits=ttw, inc="sV", name="D.tt.%d.%d.%d" % (tcg, og, tci))
                op("vector", lambda e, tr=tr, og=og, tci=tci: e.reduce_max(
                    mcols[:, tci * 16 + og: tci * 16 + og + 1], tr[:, og * 512:(og + 1) * 512],
                    axis=AX, apply_absolute_value=True),
                   waits=["D.tt.%d.%d.%d" % (tcg, og, tci)], inc="sV",
                   name="D.mx.%d.%d.%d" % (tcg, og, tci))
            mark("D.wused.%d.%d" % (tcg, og), "sT")
        mark("D.mmlast.%d" % tcg, "sT")
        for tci in range(2):
            tr = trows[tci]
            qtrow = qtr2[tci]
            op("vector", lambda e, tci=tci: e.reduce_max(
                qsm[:, tci * 4:tci * 4 + 1], mcols[:, tci * 16:tci * 16 + NG], axis=AX),
               waits=["D.mx.%d.%d.%d" % (tcg, NG - 1, tci)], inc="sV",
               name="D.m.%d.%d" % (tcg, tci))
            op("vector", lambda e, tci=tci: e.tensor_scalar_mul(
                qsm[:, tci * 4 + 1:tci * 4 + 2], qsm[:, tci * 4:tci * 4 + 1], 1.0 / 224.0),
               waits=["D.m.%d.%d" % (tcg, tci)], inc="sV", name="D.msc.%d.%d" % (tcg, tci))
            op("vector", lambda e, tci=tci: e.reciprocal(
                qsm[:, tci * 4 + 2:tci * 4 + 3], qsm[:, tci * 4 + 1:tci * 4 + 2]),
               waits=["D.msc.%d.%d" % (tcg, tci)], inc="sV", name="D.minv.%d.%d" % (tcg, tci))
            op("scalar", lambda e, tr=tr, tci=tci: e.activation(
                out=q8r[:], in_=tr[:], func=AF.Copy, scale=qsm[:, tci * 4 + 2:tci * 4 + 3]),
               waits=["D.minv.%d.%d" % (tcg, tci),
                      ("D.qt.%d.%d" % (tcg, tci - 1)) if tci >= 1 else
                      (("D.qt.%d.1" % (tcg - 1)) if tcg >= 1 else None)],
               inc="sS", name="D.q8.%d.%d" % (tcg, tci))
            op("scalar", lambda e, qtrow=qtrow, tci=tci: e.activation(
                out=qtrow[:], in_=q8r[:], func=AF.Copy, scale=qsm[:, tci * 4 + 1:tci * 4 + 2]),
               waits=["D.q8.%d.%d" % (tcg, tci),
                      ("D.qtw.%d.%d" % (tcg - 1, tci)) if tcg >= 1 else None],
               inc="sS", name="D.qt.%d.%d" % (tcg, tci))
            op("gpsimd", lambda e, tcg=tcg, tci=tci, qtrow=qtrow: e.dma_start(
                out=qt_d[tcg * 256 + tci * 128: tcg * 256 + (tci + 1) * 128, :], in_=qtrow[:]),
               waits=["D.qt.%d.%d" % (tcg, tci)], inc="dQW%d" % tci, inc_amt=16,
               name="D.qtw.%d.%d" % (tcg, tci))

    barrier("D")
    esD.close()

    esE = ExitStack(); esE.__enter__()
    qtt_sb = sbuf(esE, "qtt_sb", [128, 44 * 512], BF16)
    wdg_sb = sbuf(esE, "wdg_sb", [128, 11 * 2048], BF16)
    yac_sb = sbuf(esE, "yac_sb", [128, 4 * 2048], F32)
    gf2 = [sbuf(esE, "gf%d" % i, [128, 512], BF16) for i in range(2)]
    st16 = sbuf(esE, "st16", [128, 4 * 2048], BF16)
    xt16 = sbuf(esE, "xt16", [128, 4 * 2048], F32)
    tv2 = [sbuf(esE, "tv2%d" % i, [128, 512], F32) for i in range(2)]
    ot2 = [sbuf(esE, "ot2%d" % i, [128, 512], F32) for i in range(2)]

    NOC = I_FFN // 128
    for sup in range(4):
        for oc in range(NOC):
            op("scalar", lambda e, sup=sup, oc=oc: e.dma_start(
                out=qtt_sb[:, oc * 512:(oc + 1) * 512],
                in_=qt_d[sup * 512:(sup + 1) * 512, oc * 128:(oc + 1) * 128], transpose=True),
               waits=[("E.mmdone.%d" % (sup - 1)) if (oc == 0 and sup >= 1) else None],
               inc="dQT", inc_amt=16)
        mark("E.qtt.%d" % sup, "dQT")
        for tc in range(4):
            r0 = sup * 512 + tc * 128
            op("sync", lambda e, r0=r0, tc=tc: e.dma_start(
                out=st16[:, tc * 2048:(tc + 1) * 2048], in_=sat_d[r0:r0 + 128, :]),
               waits=[("E.cmb1.%d.3.3" % (sup - 1)) if (tc == 0 and sup >= 1) else None],
               inc="dCB0", inc_amt=16)
            op("sync", lambda e, r0=r0, tc=tc: e.dma_start(
                out=xt16[:, tc * 2048:(tc + 1) * 2048], in_=x_d[r0:r0 + 128, :]),
               inc="dCB1", inc_amt=16)
        cp["E.sx.%d" % sup] = ("dCB1", ctr["dCB1"])
        cp["E.sx0.%d" % sup] = ("dCB0", ctr["dCB0"])
        for ocg in range(4):
            for j in range(11):
                oc = ocg * 11 + j
                if j == 0:
                    if ocg >= 1:
                        wddep = "E.accv.%d.%d" % (sup, ocg - 1)
                    elif sup >= 1:
                        wddep = "E.accv.%d.3" % (sup - 1)
                    else:
                        wddep = None
                else:
                    wddep = None
                op("sync", lambda e, oc=oc, j=j: e.dma_start(
                    out=wdg_sb[:, j * 2048:(j + 1) * 2048], in_=wdT_d[oc * 128:(oc + 1) * 128, :]),
                   waits=[wddep], inc="dWD", inc_amt=16)
            mark("E.wd.%d.%d" % (sup, ocg), "dWD")
            for tc in range(4):
                fi = ocg * 4 + tc
                gfi = sup * 16 + fi
                if gfi >= 2:
                    pfi = fi - 2 if fi >= 2 else fi + 14
                    psup = sup if fi >= 2 else sup - 1
                    bdep = "E.vac.%d.%d.%d" % (psup, pfi // 4, pfi % 4)
                else:
                    bdep = None
                gb = (fi % 2) * 4
                for j in range(11):
                    for og in range(4):
                        op("tensor", lambda e, ocg=ocg, tc=tc, j=j, og=og, gb=gb: e.matmul(
                            psb[gb + og],
                            qtt_sb[:, (ocg * 11 + j) * 512 + tc * 128: (ocg * 11 + j) * 512 + tc * 128 + 128],
                            wdg_sb[:, j * 2048 + og * 512: j * 2048 + og * 512 + 512],
                            start=(j == 0), stop=(j == 10)),
                           waits=([("E.qtt.%d" % sup), ("E.wd.%d.%d" % (sup, ocg)), bdep]
                                  if (j == 0 and og == 0) else []),
                           inc="sT" if j == 10 else None,
                           name=("E.ymm.%d.%d.%d.%d" % (sup, ocg, tc, og)) if j == 10 else None)
                for og in range(4):
                    if ocg == 0:
                        yw = ["E.ymm.%d.%d.%d.%d" % (sup, ocg, tc, og)]
                        if sup >= 1:
                            yw.append("E.gfall.%d" % (sup - 1))
                        op("vector", lambda e, tc=tc, og=og, gb=gb: e.tensor_copy(
                            out=yac_sb[:, tc * 2048 + og * 512: tc * 2048 + og * 512 + 512],
                            in_=psb[gb + og]),
                           waits=yw, inc="sV", name="E.yac.%d.%d.%d.%d" % (sup, ocg, tc, og))
                    else:
                        op("vector", lambda e, tc=tc, og=og, gb=gb: e.tensor_tensor(
                            out=yac_sb[:, tc * 2048 + og * 512: tc * 2048 + og * 512 + 512],
                            in0=yac_sb[:, tc * 2048 + og * 512: tc * 2048 + og * 512 + 512],
                            in1=psb[gb + og], op=ALU.add),
                           waits=["E.ymm.%d.%d.%d.%d" % (sup, ocg, tc, og)],
                           inc="sV", name="E.yac.%d.%d.%d.%d" % (sup, ocg, tc, og))
                mark("E.vac.%d.%d.%d" % (sup, ocg, tc), "sV")
            mark("E.accv.%d.%d" % (sup, ocg), "sV")
        mark("E.mmdone.%d" % sup, "sT")
        for tc in range(4):
            for og in range(4):
                fl = tc * 4 + og
                pi = fl % 2
                gf, tv, ot = gf2[pi], tv2[pi], ot2[pi]
                r0 = sup * 512 + tc * 128
                if fl >= 2:
                    ptc, pog = divmod(fl - 2, 4)
                    bufdep = "E.cmb1.%d.%d.%d" % (sup, ptc, pog)
                elif sup >= 1:
                    bufdep = "E.cmb1.%d.3.%d" % (sup - 1, 2 + og)
                else:
                    bufdep = None
                op("scalar", lambda e, tc=tc, og=og, gf=gf: e.activation(
                    out=gf[:], in_=yac_sb[:, tc * 2048 + og * 512: tc * 2048 + og * 512 + 512],
                    func=AF.Silu),
                   waits=["E.accv.%d.3" % sup, bufdep],
                   inc="sS", name="E.gf.%d.%d.%d" % (sup, tc, og))
                op("vector", lambda e, gf=gf, tv=tv, tc=tc, og=og: e.tensor_tensor(
                    out=tv[:], in0=gf[:], in1=st16[:, tc * 2048 + og * 512: tc * 2048 + og * 512 + 512],
                    op=ALU.mult),
                   waits=["E.sx0.%d" % sup, "E.gf.%d.%d.%d" % (sup, tc, og)],
                   inc="sV", name="E.tv.%d.%d.%d" % (sup, tc, og))
                if fl >= 2:
                    podep = "E.od.%d.%d.%d" % (sup, ptc, pog)
                elif sup >= 1:
                    podep = "E.od.%d.3.%d" % (sup - 1, 2 + og)
                else:
                    podep = None
                op("vector", lambda e, tv=tv, ot=ot, tc=tc, og=og: e.tensor_tensor(
                    out=ot[:], in0=tv[:], in1=xt16[:, tc * 2048 + og * 512: tc * 2048 + og * 512 + 512],
                    op=ALU.add),
                   waits=["E.sx.%d" % sup, "E.tv.%d.%d.%d" % (sup, tc, og), podep],
                   inc="sV", name="E.cmb1.%d.%d.%d" % (sup, tc, og))
                op("sync", lambda e, r0=r0, og=og, ot=ot: e.dma_start(
                    out=out_d[r0:r0 + 128, og * 512:(og + 1) * 512], in_=ot[:]),
                   waits=["E.cmb1.%d.%d.%d" % (sup, tc, og)], inc="dO%d" % (fl % 2), inc_amt=16,
                   name="E.od.%d.%d.%d" % (sup, tc, og))
        mark("E.gfall.%d" % sup, "sS")

    op("sync", lambda e: None, waits=[("dO0", ctr["dO0"]), ("dO1", ctr["dO1"])])
    op("gpsimd", lambda e: None, waits=[("dQW0", ctr["dQW0"]), ("dQW1", ctr["dQW1"])])
    flush()
    esE.close()
    es_top.close()
    return nc


_NC_CACHE = None

def kernel(L0_attn_x, L0_attn_w_qkv, L0_attn_w_o, L0_ffn_w_gate, L0_ffn_w_up,
           L0_ffn_w_down, L0_ffn_up_reorder_index, L0_ffn_down_reorder_index):
    global _NC_CACHE
    x = np.ascontiguousarray(np.asarray(L0_attn_x, dtype=np.float32))
    w_qkv = np.ascontiguousarray(np.asarray(L0_attn_w_qkv, dtype=np.float32))
    w_o = np.ascontiguousarray(np.asarray(L0_attn_w_o, dtype=np.float32))
    w_g = np.ascontiguousarray(np.asarray(L0_ffn_w_gate, dtype=np.float32))
    w_u = np.ascontiguousarray(np.asarray(L0_ffn_w_up, dtype=np.float32))
    w_d = np.asarray(L0_ffn_w_down, dtype=np.float32)
    down_idx = np.asarray(L0_ffn_down_reorder_index)
    wd_perm = np.ascontiguousarray(w_d[:, np.argsort(down_idx)])
    wv = np.ascontiguousarray(w_qkv[:, 4096:6144])

    if _NC_CACHE is None:
        _NC_CACHE = build_nc()
    nc = _NC_CACHE

    in_maps = []
    for c in range(8):
        b, p = c // 2, c % 2
        xb = x[b]
        xrot = np.ascontiguousarray(np.concatenate([xb[S_OWN * p:], xb[:S_OWN * p]], axis=0)) if p else xb
        wqk = np.ascontiguousarray(np.concatenate(
            [w_qkv[:, 1024 * p:1024 * p + 1024],
             w_qkv[:, 2048 + 1024 * p:2048 + 1024 * p + 1024]], axis=1))
        S = np.zeros((128, 16), np.float32)
        for h in range(16):
            r_h = 2 * b + h // 8
            half = 0 if p == (r_h % 2) else 1
            S[16 * r_h + (h % 8) + 8 * half, h] = 1.0
        in_maps.append({
            "x": xrot, "wqk": wqk, "wv": wv, "wo": w_o,
            "wg": w_g, "wu": w_u, "wd": wd_perm, "sel": S,
        })
    res = run_bass_kernel_spmd(nc, in_maps, core_ids=list(range(8)))
    out = np.empty((4, S_FULL, D), np.float32)
    for c in range(8):
        b, p = c // 2, c % 2
        out[b, S_OWN * p:S_OWN * (p + 1)] = res.results[c]["out"]
    return out


# version 19 (final)
# speedup vs baseline: 1.1499x; 1.0071x over previous
import numpy as np
from contextlib import ExitStack

import concourse.bass as bass
import concourse.mybir as mybir
from concourse.bass_utils import run_bass_kernel_spmd

F32 = mybir.dt.float32
BF16 = mybir.dt.bfloat16
FP8 = mybir.dt.float8e4
AF = mybir.ActivationFunctionType
ALU = mybir.AluOpType
AX = mybir.AxisListType.X

S_FULL, S_OWN, D = 4096, 2048, 2048
NH, DH, NHO = 16, 128, 8
I_FFN = 5632
CSH = 20.0
ASC = float(1.0 / np.sqrt(np.float64(DH)))
ENGINES = ("sync", "tensor", "scalar", "vector", "gpsimd")


def build_nc():
    nc = bass.Bass("TRN2", target_bir_lowering=False)

    x_d = nc.dram_tensor("x", [S_FULL, D], F32, kind="ExternalInput").ap()
    wqk_d = nc.dram_tensor("wqk", [D, 2048], F32, kind="ExternalInput").ap()
    wv_d = nc.dram_tensor("wv", [D, D], F32, kind="ExternalInput").ap()
    wo_d = nc.dram_tensor("wo", [D, D], F32, kind="ExternalInput").ap()
    wg_d = nc.dram_tensor("wg", [I_FFN, D], F32, kind="ExternalInput").ap()
    wu_d = nc.dram_tensor("wu", [I_FFN, D], F32, kind="ExternalInput").ap()
    wd_d = nc.dram_tensor("wd", [D, I_FFN], F32, kind="ExternalInput").ap()
    sel_d = nc.dram_tensor("sel", [128, 16], F32, kind="ExternalInput").ap()
    out_d = nc.dram_tensor("out", [S_OWN, D], F32, kind="ExternalOutput").ap()

    xn16_d = nc.dram_tensor("xn16_s", [S_FULL, D], BF16).ap()
    xq_d = nc.dram_tensor("xq_s", [S_OWN, D], BF16).ap()
    weffg_d = nc.dram_tensor("weffg_s", [I_FFN, D], BF16).ap()
    weffu_d = nc.dram_tensor("weffu_s", [I_FFN, D], BF16).ap()
    weffd_d = nc.dram_tensor("weffd_s", [D, I_FFN], BF16).ap()
    wgT_d = nc.dram_tensor("wgT_s", [D, I_FFN], BF16).ap()
    wuT_d = nc.dram_tensor("wuT_s", [D, I_FFN], BF16).ap()
    wdT_d = nc.dram_tensor("wdT_s", [I_FFN, D], BF16).ap()
    qt_d = nc.dram_tensor("qt_s", [S_OWN, I_FFN], BF16).ap()
    sat_d = nc.dram_tensor("sat_s", [S_OWN, D], BF16).ap()
    cc_in = nc.dram_tensor("cc_in", [16, S_OWN], F32).ap()
    c_dram = nc.dram_tensor("c_dram", [16, S_OWN], F32).ap()
    cc_out = nc.dram_tensor("cc_out", [128, S_OWN], F32).ap()

    ctr = {}
    cp = {}
    prog = {k: [] for k in ENGINES}

    es_top = ExitStack()
    es_top.__enter__()

    def sbuf(stack, name, shape, dt, side=None):
        return stack.enter_context(nc.sbuf_tensor(name, shape, dt, side=side))

    ones_f = sbuf(es_top, "ones_f", [128, 1], F32)
    cbias = sbuf(es_top, "cbias", [128, 2], F32)
    ones_r = sbuf(es_top, "ones_r", [1, 128], F32)
    sel_sb = sbuf(es_top, "sel_sb", [128, 16], F32)
    c_sb = sbuf(es_top, "c_sb", [16, S_OWN], F32)
    crow = sbuf(es_top, "crow", [1, S_FULL], F32)

    psA = es_top.enter_context(nc.psum_tensor("psA", [128, 2048], F32))
    psB = es_top.enter_context(nc.psum_tensor("psB", [128, 2048], F32))
    psb = [psA[:, i * 512:(i + 1) * 512] for i in range(4)] + \
          [psB[:, i * 512:(i + 1) * 512] for i in range(4)]

    sems = {}
    for s in ("sT", "sS", "sV", "sG", "sC", "bG",
              "dSel", "dX0", "dX1", "dPW0", "dPW1", "dTW0", "dTW1", "dXT",
              "dOT0", "dOT1", "dQX0", "dQX1", "dWG0", "dWG1", "dQT", "dWD",
              "dCB0", "dCB1", "dO0", "dO1",
              "dN0", "dN1", "dXQ0", "dXQ1", "dWS0", "dWS1", "dTS0", "dTS1",
              "dQK0", "dQK1", "dCC", "dWV0", "dWV1", "dCS", "dCD",
              "dC10", "dC11", "dWO0", "dWO1", "dSA0", "dSA1", "dQW0", "dQW1"):
        sems[s] = es_top.enter_context(nc.semaphore(s))

    def op(eng, fn, waits=(), inc=None, inc_amt=1, name=None):
        wl = []
        for w in waits:
            if w is None:
                continue
            if isinstance(w, str):
                wl.append(cp[w])
            else:
                wl.append(w)
        if inc is not None:
            ctr[inc] = ctr.get(inc, 0) + inc_amt
            val = ctr[inc]
        def run(e, fn=fn, wl=tuple(wl), inc=inc, inc_amt=inc_amt):
            for (s, v) in wl:
                if v > 0:
                    e.wait_ge(sems[s], v)
            inst = fn(e)
            if inc is not None:
                if inst is None:
                    inst = e.nop()
                inst.then_inc(sems[inc], inc_amt)
        prog[eng].append(run)
        if name is not None:
            assert inc is not None
            cp[name] = (inc, val)

    def mark(name, sem):
        cp[name] = (sem, ctr.get(sem, 0))

    block = es_top.enter_context(nc.Block())

    def flush():
        for name in ENGINES:
            lst = prog[name]
            if not lst:
                continue
            prog[name] = []
            getattr(block, name)(lambda e, lst=tuple(lst): [f(e) for f in lst])

    def barrier(tag):
        targets = [(s, ctr.get(s, 0)) for s in sems
                   if s[0] == "d" and ctr.get(s, 0) > 0]
        for name in ENGINES:
            op(name, lambda e: None, waits=targets)
        for name in ("tensor", "scalar", "vector", "gpsimd"):
            op(name, lambda e: e.drain(fusable=False))
        for name in ENGINES:
            op(name, lambda e: None, inc="bG")
        arrive = ctr["bG"]
        for name in ENGINES:
            op(name, lambda e: None, waits=[("bG", arrive)])
        flush()

    HW = 2816
    esPW = ExitStack(); esPW.__enter__()
    wqf2 = [sbuf(esPW, "wqf%d" % i, [128, HW], F32) for i in range(2)]
    wq8 = sbuf(esPW, "wq8", [128, HW], FP8)
    wqb2 = [sbuf(esPW, "wqb%d" % i, [128, HW], BF16) for i in range(2)]
    wtb2 = [sbuf(esPW, "wtb%d" % i, [128, HW], BF16) for i in range(2)]
    wsc = sbuf(esPW, "wsc", [128, 8], F32)


    pw_thunks = []
    tw_thunks = []

    def make_qw(tag, src_d, dst_d, nchunks, width, prevtag, nprev):
        for k in range(nchunks):
            def thunk(k=k, tag=tag, src_d=src_d, dst_d=dst_d, width=width,
                      prevtag=prevtag, nprev=nprev):
                wf, wb = wqf2[k % 2], wqb2[k % 2]
                op("sync", lambda e, k=k, wf=wf: e.dma_start(
                    out=wf[:, :width], in_=src_d[k * 128:(k + 1) * 128, :]),
                   waits=[("%s.bf.%d" % (tag, k - 2)) if k >= 2 else
                          (("%s.bf.last" % prevtag) if prevtag else None)],
                   inc="dPW%d" % (k % 2), inc_amt=16, name="%s.ld.%d" % (tag, k))
                op("vector", lambda e, wf=wf: e.reduce_max(
                    wsc[:, 0:1], wf[:, :width], axis=AX, apply_absolute_value=True),
                   waits=["%s.ld.%d" % (tag, k),
                          ("%s.bf.%d" % (tag, k - 1)) if k >= 1 else
                          (("%s.bf.last" % prevtag) if prevtag else None)],
                   inc="sV", name="%s.m.%d" % (tag, k))
                op("vector", lambda e: e.tensor_scalar_mul(wsc[:, 1:2], wsc[:, 0:1], 1.0 / 224.0),
                   waits=["%s.m.%d" % (tag, k)], inc="sV", name="%s.msc.%d" % (tag, k))
                op("vector", lambda e: e.reciprocal(wsc[:, 2:3], wsc[:, 1:2]),
                   waits=["%s.msc.%d" % (tag, k)], inc="sV", name="%s.minv.%d" % (tag, k))
                op("scalar", lambda e, wf=wf: e.activation(
                    out=wq8[:, :width], in_=wf[:, :width], func=AF.Copy, scale=wsc[:, 2:3]),
                   waits=["%s.minv.%d" % (tag, k),
                          ("%s.bf.%d" % (tag, k - 1)) if k >= 1 else
                          (("%s.bf.last" % prevtag) if prevtag else None)],
                   inc="sS", name="%s.q8.%d" % (tag, k))
                op("scalar", lambda e, wb=wb: e.activation(
                    out=wb[:, :width], in_=wq8[:, :width], func=AF.Copy, scale=wsc[:, 1:2]),
                   waits=["%s.q8.%d" % (tag, k),
                          ("%s.st.%d" % (tag, k - 2)) if k >= 2 else
                          (("%s.st.%d" % (prevtag, nprev - 2 + k)) if prevtag else None)],
                   inc="sS", name="%s.bf.%d" % (tag, k))
                op("gpsimd", lambda e, k=k, wb=wb: e.dma_start(
                    out=dst_d[k * 128:(k + 1) * 128, :], in_=wb[:, :width]),
                   waits=["%s.bf.%d" % (tag, k)], inc="dWS%d" % (k % 2), inc_amt=16,
                   name="%s.st.%d" % (tag, k))
                if k == nchunks - 1:
                    cp["%s.bf.last" % tag] = cp["%s.bf.%d" % (tag, k)]
            pw_thunks.append(thunk)

    def make_qw_halves(tag, src_d, dst_d, nchunks, width, prevtag, nprev):
        half = width // 2
        for k in range(nchunks):
            def thunk(k=k, tag=tag, src_d=src_d, dst_d=dst_d, half=half,
                      prevtag=prevtag, nprev=nprev):
                prev_bf = ("%s.bf1.%d" % (tag, k - 1)) if k >= 1 else                           (("%s.bf.last" % prevtag) if prevtag else None)
                for hx in range(2):
                    op("sync", lambda e, k=k, hx=hx: e.dma_start(
                        out=wqf2[hx][:, :half],
                        in_=src_d[k * 128:(k + 1) * 128, hx * half:(hx + 1) * half]),
                       waits=[prev_bf if hx == 0 else None],
                       inc="dPW%d" % hx, inc_amt=16, name="%s.ld%d.%d" % (tag, hx, k))
                    op("vector", lambda e, hx=hx: e.reduce_max(
                        wsc[:, hx * 3:hx * 3 + 1], wqf2[hx][:, :half],
                        axis=AX, apply_absolute_value=True),
                       waits=["%s.ld%d.%d" % (tag, hx, k), prev_bf],
                       inc="sV", name="%s.m%d.%d" % (tag, hx, k))
                op("vector", lambda e: e.tensor_max(wsc[:, 6:7], wsc[:, 0:1], wsc[:, 3:4]),
                   waits=["%s.m0.%d" % (tag, k), "%s.m1.%d" % (tag, k)],
                   inc="sV", name="%s.mm.%d" % (tag, k))
                op("vector", lambda e: e.tensor_scalar_mul(wsc[:, 1:2], wsc[:, 6:7], 1.0 / 224.0),
                   waits=["%s.mm.%d" % (tag, k)], inc="sV", name="%s.msc.%d" % (tag, k))
                op("vector", lambda e: e.reciprocal(wsc[:, 2:3], wsc[:, 1:2]),
                   waits=["%s.msc.%d" % (tag, k)], inc="sV", name="%s.minv.%d" % (tag, k))
                for hx in range(2):
                    op("scalar", lambda e, hx=hx: e.activation(
                        out=wq8[:, :half], in_=wqf2[hx][:, :half], func=AF.Copy,
                        scale=wsc[:, 2:3]),
                       waits=["%s.minv.%d" % (tag, k),
                              ("%s.bf%d.%d" % (tag, 1, k - 1)) if (hx == 0 and k >= 1) else None,
                              ("%s.bf%d.%d" % (tag, 0, k)) if hx == 1 else None],
                       inc="sS", name="%s.q8%d.%d" % (tag, hx, k))
                    op("scalar", lambda e, hx=hx: e.activation(
                        out=wqb2[hx][:, :half], in_=wq8[:, :half], func=AF.Copy,
                        scale=wsc[:, 1:2]),
                       waits=["%s.q8%d.%d" % (tag, hx, k),
                              ("%s.st%d.%d" % (tag, hx, k - 1)) if k >= 1 else
                              (("%s.st.%d" % (prevtag, nprev - 2 + hx)) if prevtag else None)],
                       inc="sS", name="%s.bf%d.%d" % (tag, hx, k))
                    op("gpsimd", lambda e, k=k, hx=hx: e.dma_start(
                        out=dst_d[k * 128:(k + 1) * 128, hx * half:(hx + 1) * half],
                        in_=wqb2[hx][:, :half]),
                       waits=["%s.bf%d.%d" % (tag, hx, k)], inc="dWS%d" % hx, inc_amt=16,
                       name="%s.st%d.%d" % (tag, hx, k))
                if k == nchunks - 1:
                    def fin(e):
                        return None
                    cp["%s.bf.last" % tag] = cp["%s.bf1.%d" % (tag, k)]
                    cp["pw.ad0"] = ("dWS0", ctr.get("dWS0", 0))
                    cp["pw.ad1"] = ("dWS1", ctr.get("dWS1", 0))
            pw_thunks.append(thunk)

    make_qw("pwg", wg_d, weffg_d, I_FFN // 128, D, None, 0)
    make_qw("pwu", wu_d, weffu_d, I_FFN // 128, D, "pwg", 44)
    make_qw_halves("pwd", wd_d, weffd_d, D // 128, I_FFN, "pwu", 44)

    def make_tw(tag, src_d, dst_d, nchunks, width, previdx):
        nh = (width + HW - 1) // HW
        idx = [previdx]
        for k in range(nchunks):
            for hx in range(nh):
                w0 = hx * HW
                w1 = min(width, w0 + HW)
                fi = idx[0]
                idx[0] += 1
                def thunk(k=k, hx=hx, w0=w0, w1=w1, fi=fi, tag=tag,
                          src_d=src_d, dst_d=dst_d):
                    wt = wtb2[fi % 2]
                    op("sync", lambda e: e.dma_start(
                        out=wt[:, :w1 - w0],
                        in_=src_d[w0:w1, k * 128:(k + 1) * 128], transpose=True),
                       waits=["pw.ad0", "pw.ad1",
                              ("tw.st.%d" % (fi - 2)) if fi - 2 >= 0 else None],
                       inc="dTW%d" % (fi % 2), inc_amt=16, name="tw.ld.%d" % fi)
                    op("gpsimd", lambda e: e.dma_start(
                        out=dst_d[k * 128:(k + 1) * 128, w0:w1], in_=wt[:, :w1 - w0]),
                       waits=["tw.ld.%d" % fi], inc="dTS%d" % (fi % 2), inc_amt=16,
                       name="tw.st.%d" % fi)
                tw_thunks.append(thunk)
        return idx[0]

    _ti = make_tw("twg", weffg_d, wgT_d, D // 128, I_FFN, 0)
    _ti = make_tw("twu", weffu_d, wuT_d, D // 128, I_FFN, _ti)
    _ti = make_tw("twd", weffd_d, wdT_d, I_FFN // 128, D, _ti)

    def drain(lst, n):
        while n > 0 and lst:
            lst.pop(0)()
            n -= 1

    esA = ExitStack(); esA.__enter__()
    xt2 = [sbuf(esA, "xt%d" % i, [128, D], F32) for i in range(2)]
    cent2 = [sbuf(esA, "cent%d" % i, [128, D], F32) for i in range(2)]
    sqt = sbuf(esA, "sqt", [128, D], F32)
    xnf2 = [sbuf(esA, "xnf%d" % i, [128, D], F32) for i in range(2)]
    xn16_2 = [sbuf(esA, "xn16_%d" % i, [128, D], BF16) for i in range(2)]
    q8t2 = [sbuf(esA, "q8t%d" % i, [128, D], FP8) for i in range(2)]
    xeff2 = [sbuf(esA, "xeff%d" % i, [128, D], BF16) for i in range(2)]
    lnsm = sbuf(esA, "lnsm", [128, 16], F32)

    op("vector", lambda e: e.memset(ones_f[:], 1.0), inc="sV")
    op("vector", lambda e: e.memset(cbias[:, 0:1], 1e-5), inc="sV")
    op("vector", lambda e: e.memset(cbias[:, 1:2], -CSH), inc="sV")
    op("vector", lambda e: e.memset(ones_r[:], 1.0), inc="sV", name="init_ones")
    op("sync", lambda e: e.dma_start(out=sel_sb[:], in_=sel_d), inc="dSel", inc_amt=16, name="sel_ld")

    NT = S_FULL // 128
    NOWN = S_OWN // 128
    for it in range(NT):
        pi = it % 2
        own = it < NOWN
        xt, cent, xnf = xt2[pi], cent2[pi], xnf2[pi]
        xn16t, q8t, xeff = xn16_2[pi], q8t2[pi], xeff2[pi]
        def L(k, pi=pi):
            return lnsm[:, 8 * pi + k: 8 * pi + k + 1]
        op("sync", lambda e, it=it, xt=xt: e.dma_start(out=xt[:], in_=x_d[it * 128:(it + 1) * 128, :]),
           waits=[("P0.lastread.%d" % (it - 2)) if it >= 2 else None],
           inc="dX%d" % (it % 2), inc_amt=16, name="P0.xd.%d" % it)
        op("vector", lambda e, xt=xt, L=L: e.reduce_sum(L(0), xt[:], axis=AX),
           waits=["P0.xd.%d" % it, ("P0.lastread.%d" % (it - 2)) if it >= 2 else None],
           inc="sV", name="P0.su.%d" % it)
        op("vector", lambda e, L=L: e.tensor_scalar_mul(L(1), L(0), 1.0 / D),
           waits=["P0.su.%d" % it], inc="sV", name="P0.mu.%d" % it)
        op("vector", lambda e, xt=xt, cent=cent, L=L: e.tensor_scalar(
            out=cent[:], in0=xt[:], scalar1=L(1), scalar2=None, op0=ALU.subtract),
           waits=["P0.mu.%d" % it], inc="sV", name="P0.cent.%d" % it)
        op("scalar", lambda e, cent=cent, L=L: e.activation(
            out=sqt[:], in_=cent[:], func=AF.Square, accum_out=L(2)),
           waits=["P0.cent.%d" % it,
                  ("P0.std.%d" % (it - 2)) if it >= 2 else None],
           inc="sS", name="P0.sq.%d" % it)
        op("scalar", lambda e, L=L: e.activation(
            out=L(4), in_=L(2), func=AF.Sqrt, scale=1.0 / D, bias=cbias[:, 0:1]),
           waits=["P0.sq.%d" % it], inc="sS", name="P0.std.%d" % it)
        op("vector", lambda e, L=L: e.reciprocal(L(5), L(4)),
           waits=["P0.std.%d" % it], inc="sV", name="P0.rstd.%d" % it)
        op("scalar", lambda e, cent=cent, xn16t=xn16t, L=L: e.activation(
            out=xn16t[:], in_=cent[:], func=AF.Copy, scale=L(5)),
           waits=["P0.rstd.%d" % it, ("P0.xnd.%d" % (it - 2)) if it >= 2 else None],
           inc="sS", name="P0.xn16.%d" % it)
        if own:
            op("scalar", lambda e, cent=cent, xnf=xnf, L=L: e.activation(
                out=xnf[:], in_=cent[:], func=AF.Copy, scale=L(5)),
               inc="sS", name="P0.lastread.%d" % it)
        else:
            cp["P0.lastread.%d" % it] = cp["P0.xn16.%d" % it]
        op("gpsimd", lambda e, it=it, xn16t=xn16t: e.dma_start(
            out=xn16_d[it * 128:(it + 1) * 128, :], in_=xn16t[:]),
           waits=["P0.xn16.%d" % it], inc="dN%d" % (it % 2), inc_amt=16, name="P0.xnd.%d" % it)
        if own:
            op("vector", lambda e, xnf=xnf, L=L: e.reduce_max(
                L(6), xnf[:], axis=AX, apply_absolute_value=True),
               waits=["P0.lastread.%d" % it], inc="sV", name="P0.m.%d" % it)
            op("vector", lambda e, L=L: e.tensor_scalar_mul(L(7), L(6), 1.0 / 224.0),
               waits=["P0.m.%d" % it], inc="sV", name="P0.msc.%d" % it)
            op("vector", lambda e, L=L: e.reciprocal(L(3), L(7)),
               waits=["P0.msc.%d" % it], inc="sV", name="P0.minv.%d" % it)
            op("scalar", lambda e, xnf=xnf, q8t=q8t, L=L: e.activation(
                out=q8t[:], in_=xnf[:], func=AF.Copy, scale=L(3)),
               waits=["P0.minv.%d" % it,
                      ("P0.xe.%d" % (it - 2)) if it >= 2 else None],
               inc="sS", name="P0.q8.%d" % it)
            op("scalar", lambda e, q8t=q8t, xeff=xeff, L=L: e.activation(
                out=xeff[:], in_=q8t[:], func=AF.Copy, scale=L(7)),
               waits=["P0.q8.%d" % it, ("P0.xqd.%d" % (it - 2)) if it >= 2 else None],
               inc="sS", name="P0.xe.%d" % it)
            op("gpsimd", lambda e, it=it, xeff=xeff: e.dma_start(
                out=xq_d[it * 128:(it + 1) * 128, :], in_=xeff[:]),
               waits=["P0.xe.%d" % it], inc="dXQ%d" % (it % 2), inc_amt=16, name="P0.xqd.%d" % it)
        if it >= 2:
            drain(pw_thunks, 3)

    barrier("A")
    esA.close()

    esB1a = ExitStack(); esB1a.__enter__()
    xnT_own = sbuf(esB1a, "xnT_own", [128, 16 * S_OWN], BF16)
    esB1b = ExitStack(); esB1b.__enter__()
    qh_sb = sbuf(esB1b, "qh_sb", [128, S_FULL], BF16, side="right")
    kh_sb = sbuf(esB1b, "kh_sb", [128, S_FULL], BF16, side="right")
    e_row2 = [sbuf(esB1b, "e_row%d" % i, [128, S_FULL], BF16, side="right") for i in range(2)]
    c_acc = sbuf(esB1b, "c_acc", [128, S_FULL], F32, side="right")
    wqh2 = [sbuf(esB1b, "wqh%d" % i, [128, 2048], BF16, side="right") for i in range(2)]
    wkh2 = [sbuf(esB1b, "wkh%d" % i, [128, 2048], BF16, side="right") for i in range(2)]
    oth2 = [sbuf(esB1b, "oth%d" % i, [128, S_OWN], BF16, side="right") for i in range(2)]
    zrow = sbuf(esB1b, "zrow", [128, 8], F32, side="right")
    zsm = sbuf(esB1b, "zsm", [128, 8], F32, side="right")

    for ic in range(16):
        op("sync", lambda e, ic=ic: e.dma_start(
            out=xnT_own[:, ic * S_OWN:(ic + 1) * S_OWN],
            in_=xn16_d[0:S_OWN, ic * 128:(ic + 1) * 128], transpose=True),
           inc="dXT", inc_amt=16)
    mark("B1.xnT", "dXT")

    for h in range(NHO):
        hp = h % 2
        wqh, wkh = wqh2[hp], wkh2[hp]
        prev_scores_done = ("P1.exdone.%d" % (h - 1)) if h >= 1 else None
        prev_cred_done = ("P1.cred.%d" % (h - 1)) if h >= 1 else None
        op("gpsimd", lambda e, h=h, wqh=wqh: e.dma_start(
            out=wqh[:].rearrange("p (ic c) -> p ic c", c=128),
            in_=wqk_d[:, h * 128:(h + 1) * 128].rearrange("(ic p) c -> p ic c", p=128)),
           waits=[("P1.qkdone.%d" % (h - 2)) if h >= 2 else None],
           inc="dQK%d" % (h % 2), inc_amt=16, name="P1.wq.%d" % h)
        op("gpsimd", lambda e, h=h, wkh=wkh: e.dma_start(
            out=wkh[:].rearrange("p (ic c) -> p ic c", c=128),
            in_=wqk_d[:, 1024 + h * 128:1024 + (h + 1) * 128].rearrange("(ic p) c -> p ic c", p=128)),
           inc="dQK%d" % (h % 2), inc_amt=16, name="P1.wk.%d" % h)

        for part, w_sb, dst in (("q", wqh, qh_sb), ("k", wkh, kh_sb)):
            for tch in range(4):
                bank = (0 if part == "q" else 2) + tch % 2
                if tch >= 2:
                    bankdep = "P1.ev_%s.%d.%d" % (part, h, tch - 2)
                elif h >= 1:
                    bankdep = "P1.ev_%s.%d.%d" % (part, h - 1, tch + 2)
                else:
                    bankdep = None
                for ic in range(16):
                    op("tensor", lambda e, w_sb=w_sb, ic=ic, tch=tch, bank=bank: e.matmul(
                        psb[bank], w_sb[:, ic * 128:(ic + 1) * 128],
                        xnT_own[:, ic * S_OWN + tch * 512: ic * S_OWN + tch * 512 + 512],
                        start=(ic == 0), stop=(ic == 15)),
                       waits=([("P1.wk.%d" % h), "B1.xnT",
                               prev_scores_done, prev_cred_done, bankdep]
                              if ic == 0 else []),
                       inc="sT" if ic == 15 else None,
                       name=("P1.mm_%s.%d.%d" % (part, h, tch)) if ic == 15 else None)
                op("scalar", lambda e, dst=dst, tch=tch, bank=bank: e.copy(
                    out=dst[:, tch * 512:tch * 512 + 512], in_=psb[bank]),
                   waits=["P1.mm_%s.%d.%d" % (part, h, tch)],
                   inc="sS", name="P1.ev_%s.%d.%d" % (part, h, tch))
        for ic in range(16):
            oth = oth2[ic % 2]
            op("scalar", lambda e, h=h, ic=ic, oth=oth: e.dma_start(
                out=oth[:], in_=xn16_d[S_OWN:S_FULL, ic * 128:(ic + 1) * 128], transpose=True),
               waits=[("P1.othmm.%d.%d" % (h, ic - 2)) if ic >= 2 else
                      (("P1.oevdone.%d" % (h - 1)) if h >= 1 else None)],
               inc="dOT%d" % (ic % 2), inc_amt=16, name="P1.oth.%d.%d" % (h, ic))
            first_waits = ["P1.oth.%d.%d" % (h, ic)]
            if ic == 0:
                first_waits += ["P1.ev_q.%d.3" % h, "P1.ev_k.%d.3" % h]
                if h >= 1:
                    first_waits += ["P1.oevdone.%d" % (h - 1), "P1.crowdone.%d" % (h - 1)]
            nmm = 0
            for part, w_sb in (("q", wqh), ("k", wkh)):
                for tch in range(4):
                    bank = (0 if part == "q" else 4) + tch
                    nmm += 1
                    last = (nmm == 8)
                    op("tensor", lambda e, w_sb=w_sb, oth=oth, ic=ic, tch=tch, bank=bank: e.matmul(
                        psb[bank], w_sb[:, ic * 128:(ic + 1) * 128],
                        oth[:, tch * 512:tch * 512 + 512],
                        start=(ic == 0), stop=(ic == 15)),
                       waits=first_waits if (part == "q" and tch == 0) else [],
                       inc="sT" if last else None,
                       name=("P1.othmm.%d.%d" % (h, ic)) if last else None)
        for part, dst in (("q", qh_sb), ("k", kh_sb)):
            for tch in range(4):
                bank = (0 if part == "q" else 4) + tch
                op("scalar", lambda e, dst=dst, tch=tch, bank=bank: e.copy(
                    out=dst[:, 2048 + tch * 512: 2048 + tch * 512 + 512], in_=psb[bank]),
                   waits=["P1.othmm.%d.15" % h], inc="sS",
                   name="P1.oev_%s.%d.%d" % (part, h, tch))
        mark("P1.oevdone.%d" % h, "sS")
        mark("P1.qkdone.%d" % h, "sS")

        for qt in range(32):
            qp = qt % 2
            e_row = e_row2[qp]
            zc = zrow[:, qp * 4: qp * 4 + 2]
            zs0 = zsm[:, qp * 4: qp * 4 + 1]
            zs1 = zsm[:, qp * 4 + 1: qp * 4 + 2]
            for kc in range(8):
                bank = kc
                mmw = []
                if qt == 0 and kc == 0:
                    mmw.append("P1.qkdone.%d" % h)
                if qt >= 1 and kc % 4 == 0:
                    mmw.append("P1.ex%s.%d.%d" % ("A" if kc < 4 else "B", h, qt - 1))
                op("tensor", lambda e, qt=qt, kc=kc, bank=bank: e.matmul(
                    psb[bank], qh_sb[:, qt * 128:(qt + 1) * 128],
                    kh_sb[:, kc * 512:kc * 512 + 512], start=True, stop=True),
                   waits=mmw, inc="sT" if kc % 4 == 3 else None,
                   name=("P1.sc%s.%d.%d" % ("A" if kc < 4 else "B", h, qt)) if kc % 4 == 3 else None)
                if kc == 3:
                    exw = ["P1.scA.%d.%d" % (h, qt)]
                    if qt >= 2:
                        exw.append("P1.cacc.%d.%d" % (h, qt - 2))
                        exw.append("P1.z.%d.%d" % (h, qt - 2))
                    elif h >= 1:
                        exw.append("P1.cacc.%d.31" % (h - 1))
                        exw.append("P1.z.%d.31" % (h - 1))
                    op("scalar", lambda e, e_row=e_row, zc=zc: e.activation(
                        out=e_row[:, 0:2048], in_=psA[:, :], func=AF.Exp,
                        scale=ASC, bias=cbias[:, 1:2], accum_out=zc[:, 0:1]),
                       waits=exw, inc="sS", name="P1.exA.%d.%d" % (h, qt))
                if kc == 7:
                    op("scalar", lambda e, e_row=e_row, zc=zc: e.activation(
                        out=e_row[:, 2048:4096], in_=psB[:, :], func=AF.Exp,
                        scale=ASC, bias=cbias[:, 1:2], accum_out=zc[:, 1:2]),
                       waits=["P1.scB.%d.%d" % (h, qt)], inc="sS",
                       name="P1.exB.%d.%d" % (h, qt))
            op("vector", lambda e, zc=zc, zs0=zs0: e.reduce_sum(zs0, zc, axis=AX),
               waits=["P1.exB.%d.%d" % (h, qt)], inc="sV", name="P1.zs.%d.%d" % (h, qt))
            op("vector", lambda e, zs0=zs0, zs1=zs1: e.reciprocal(zs1, zs0),
               waits=["P1.zs.%d.%d" % (h, qt)], inc="sV", name="P1.z.%d.%d" % (h, qt))
            if qt == 0:
                op("vector", lambda e, e_row=e_row, zs1=zs1: e.tensor_scalar(
                    out=c_acc[:], in0=e_row[:], scalar1=zs1, scalar2=None, op0=ALU.mult),
                   waits=["P1.z.%d.0" % h, prev_cred_done], inc="sV", name="P1.cacc.%d.0" % h)
            else:
                op("vector", lambda e, e_row=e_row, zs1=zs1: e.scalar_tensor_tensor(
                    out=c_acc[:], in0=e_row[:], scalar=zs1, in1=c_acc[:],
                    op0=ALU.mult, op1=ALU.add),
                   waits=["P1.z.%d.%d" % (h, qt)], inc="sV", name="P1.cacc.%d.%d" % (h, qt))
        for kc in range(8):
            op("tensor", lambda e, kc=kc: e.matmul(
                psB[0:1, 0:512], ones_f[:], c_acc[:, kc * 512:kc * 512 + 512],
                start=True, stop=True),
               waits=["P1.cacc.%d.31" % h, "init_ones",
                      ("P1.crow.%d.%d" % (h, kc - 1)) if kc >= 1 else None],
               inc="sT", name="P1.cr.%d.%d" % (h, kc))
            op("scalar", lambda e, kc=kc: e.copy(
                out=crow[0:1, kc * 512:kc * 512 + 512], in_=psB[0:1, 0:512]),
               waits=["P1.cr.%d.%d" % (h, kc),
                      ("P1.ccw.%d" % (h - 1)) if (kc == 0 and h >= 1) else None],
               inc="sS", name="P1.crow.%d.%d" % (h, kc))
        mark("P1.cred.%d" % h, "sT")
        mark("P1.crowdone.%d" % h, "sS")
        mark("P1.exdone.%d" % h, "sS")
        op("gpsimd", lambda e, h=h: e.dma_start(out=cc_in[h:h + 1, :], in_=crow[0:1, 0:S_OWN]),
           waits=["P1.crow.%d.7" % h], inc="dCC", inc_amt=16)
        op("gpsimd", lambda e, h=h: e.dma_start(out=cc_in[h + 8:h + 9, :], in_=crow[0:1, S_OWN:S_FULL]),
           inc="dCC", inc_amt=16, name="P1.ccw.%d" % h)
        drain(pw_thunks, 8)

    drain(pw_thunks, 1000)
    op("gpsimd", lambda e: e.collective_compute(
        "AllGather", ALU.bypass, replica_groups=[list(range(8))], ins=[cc_in], outs=[cc_out]),
       waits=[("dCC", ctr["dCC"])], inc="sC", name="CC.done")

    barrier("B1")
    esB1b.close()

    esB2v = ExitStack(); esB2v.__enter__()
    v_sb = sbuf(esB2v, "v_sb", [128, 16 * S_OWN], BF16, side="right")
    esB2w = ExitStack(); esB2w.__enter__()
    wvc2 = [sbuf(esB2w, "wvc%d" % i, [128, 2048], BF16, side="right") for i in range(2)]

    for vc in range(16):
        wvc = wvc2[vc % 2]
        op("gpsimd", lambda e, vc=vc, wvc=wvc: e.dma_start(
            out=wvc[:].rearrange("p (ic c) -> p ic c", c=128),
            in_=wv_d[:, vc * 128:(vc + 1) * 128].rearrange("(ic p) c -> p ic c", p=128)),
           waits=[("PV.mmall.%d" % (vc - 2)) if vc >= 2 else None],
           inc="dWV%d" % (vc % 2), inc_amt=16, name="PV.wv.%d" % vc)
        for tch in range(4):
            bank = tch % 2
            if tch >= 2:
                bankdep = "PV.ev.%d.%d" % (vc, tch - 2)
            elif vc >= 1:
                bankdep = "PV.ev.%d.%d" % (vc - 1, tch + 2)
            else:
                bankdep = None
            for ic in range(16):
                op("tensor", lambda e, wvc=wvc, ic=ic, tch=tch, bank=bank: e.matmul(
                    psb[bank], wvc[:, ic * 128:(ic + 1) * 128],
                    xnT_own[:, ic * S_OWN + tch * 512: ic * S_OWN + tch * 512 + 512],
                    start=(ic == 0), stop=(ic == 15)),
                   waits=([("PV.wv.%d" % vc), bankdep] if ic == 0 else []),
                   inc="sT" if ic == 15 else None,
                   name=("PV.mm.%d.%d" % (vc, tch)) if ic == 15 else None)
            op("scalar", lambda e, vc=vc, tch=tch, bank=bank: e.copy(
                out=v_sb[:, vc * S_OWN + tch * 512: vc * S_OWN + tch * 512 + 512],
                in_=psb[bank]),
               waits=["PV.mm.%d.%d" % (vc, tch)], inc="sS", name="PV.ev.%d.%d" % (vc, tch))
        mark("PV.mmall.%d" % vc, "sT")
        drain(tw_thunks, 5)

    barrier("B2")
    esB2w.close()
    esB1a.close()

    esC = ExitStack(); esC.__enter__()
    ccs_sb = sbuf(esC, "ccs_sb", [128, S_OWN], F32)
    c1_2 = [sbuf(esC, "c1_%d" % i, [1, S_OWN], F32) for i in range(2)]
    wo2 = [sbuf(esC, "wo%d" % i, [128, 16 * 512], BF16) for i in range(2)]
    sat2 = [sbuf(esC, "sat%d" % i, [128, 512], BF16) for i in range(2)]

    op("gpsimd", lambda e: e.dma_start(out=ccs_sb[:], in_=cc_out),
       waits=["CC.done"], inc="dCS", inc_amt=16, name="C.ccs")
    for nch in range(4):
        op("tensor", lambda e, nch=nch: e.matmul(
            psB[0:16, 1024:1536], sel_sb[:], ccs_sb[:, nch * 512:(nch + 1) * 512],
            start=True, stop=True),
           waits=["C.ccs", "sel_ld", ("C.cev.%d" % (nch - 1)) if nch >= 1 else None],
           inc="sT", name="C.cmm.%d" % nch)
        op("scalar", lambda e, nch=nch: e.copy(
            out=c_sb[0:16, nch * 512:(nch + 1) * 512], in_=psB[0:16, 1024:1536]),
           waits=["C.cmm.%d" % nch], inc="sS", name="C.cev.%d" % nch)
    mark("C.csel", "sS")
    op("gpsimd", lambda e: e.dma_start(out=c_dram, in_=c_sb[0:16, :]),
       waits=["C.csel"], inc="dCD", inc_amt=16, name="C.cd")

    for hh in range(16):
        c1 = c1_2[hh % 2]
        op("gpsimd", lambda e, hh=hh, c1=c1: e.dma_start(out=c1[0:1, :], in_=c_dram[hh:hh + 1, :]),
           waits=["C.cd", ("C.bc.%d.3" % (hh - 2)) if hh >= 2 else None],
           inc="dC1%d" % (hh % 2), inc_amt=16, name="C.c1.%d" % hh)
        for nch in range(4):
            fl = hh * 4 + nch
            bank = 4 + fl % 2
            if fl >= 2:
                pt, pn = divmod(fl - 2, 4)
                bankdep = "C.vsc.%d.%d" % (pt, pn)
            else:
                bankdep = None
            op("tensor", lambda e, c1=c1, nch=nch, bank=bank: e.matmul(
                psb[bank], ones_r[0:1, :], c1[0:1, nch * 512:(nch + 1) * 512],
                start=True, stop=True),
               waits=["C.c1.%d" % hh, "init_ones", bankdep],
               inc="sT", name="C.bc.%d.%d" % (hh, nch))
            op("vector", lambda e, hh=hh, nch=nch, bank=bank: e.tensor_tensor(
                out=v_sb[:, hh * S_OWN + nch * 512: hh * S_OWN + nch * 512 + 512],
                in0=v_sb[:, hh * S_OWN + nch * 512: hh * S_OWN + nch * 512 + 512],
                in1=psb[bank], op=ALU.mult),
               waits=["C.bc.%d.%d" % (hh, nch)], inc="sV", name="C.vsc.%d.%d" % (hh, nch))
    mark("C.vall", "sV")

    for oc in range(4):
        wo_sb = wo2[oc % 2]
        op("gpsimd", lambda e, oc=oc, wo_sb=wo_sb: e.dma_start(
            out=wo_sb[:].rearrange("p (ic c) -> p ic c", c=512),
            in_=wo_d[:, oc * 512:(oc + 1) * 512].rearrange("(ic p) c -> p ic c", p=128)),
           waits=[("C.smmall.%d" % (oc - 2)) if oc >= 2 else None],
           inc="dWO%d" % (oc % 2), inc_amt=16, name="C.wo.%d" % oc)
        for tc in range(16):
            fl = oc * 16 + tc
            bank = fl % 2
            if fl >= 2:
                po, ptc = divmod(fl - 2, 16)
                bankdep = "C.sev.%d.%d" % (po, ptc)
                satdep = "C.sat.%d.%d" % (po, ptc)
            else:
                bankdep = satdep = None
            for fc in range(16):
                op("tensor", lambda e, wo_sb=wo_sb, tc=tc, fc=fc, bank=bank: e.matmul(
                    psb[bank], v_sb[:, fc * S_OWN + tc * 128:fc * S_OWN + tc * 128 + 128],
                    wo_sb[:, fc * 512:(fc + 1) * 512], start=(fc == 0), stop=(fc == 15)),
                   waits=([("C.wo.%d" % oc), "C.vall", bankdep] if fc == 0 else []),
                   inc="sT" if fc == 15 else None,
                   name=("C.smm.%d.%d" % (oc, tc)) if fc == 15 else None)
            sat = sat2[fl % 2]
            op("scalar", lambda e, sat=sat, bank=bank: e.copy(out=sat[:], in_=psb[bank]),
               waits=["C.smm.%d.%d" % (oc, tc), satdep],
               inc="sS", name="C.sev.%d.%d" % (oc, tc))
            op("gpsimd", lambda e, oc=oc, tc=tc, sat=sat: e.dma_start(
                out=sat_d[tc * 128:(tc + 1) * 128, oc * 512:(oc + 1) * 512], in_=sat[:]),
               waits=["C.sev.%d.%d" % (oc, tc)], inc="dSA%d" % ((oc * 16 + tc) % 2), inc_amt=16,
               name="C.sat.%d.%d" % (oc, tc))
        mark("C.smmall.%d" % oc, "sT")
        drain(tw_thunks, 10)

    drain(tw_thunks, 10000)
    barrier("C")
    esC.close()
    esB2v.close()
    esPW.close()

    esD = ExitStack(); esD.__enter__()
    qxg2 = [sbuf(esD, "qxg%d" % i, [128, 16 * 256], BF16) for i in range(2)]
    wgt2 = [sbuf(esD, "wgt%d" % i, [128, 16 * 512], BF16) for i in range(2)]
    wut2 = [sbuf(esD, "wut%d" % i, [128, 16 * 512], BF16) for i in range(2)]
    trows = [sbuf(esD, "trow%d" % i, [128, I_FFN], F32) for i in range(2)]
    sg2 = [sbuf(esD, "sg%d" % i, [128, 512], F32) for i in range(4)]
    mcols = sbuf(esD, "mcols", [128, 32], F32)
    q8r = sbuf(esD, "q8r", [128, I_FFN], FP8)
    qtr2 = [sbuf(esD, "qtr%d" % i, [128, I_FFN], BF16) for i in range(2)]
    qsm = sbuf(esD, "qsm", [128, 8], F32)

    NG = 11
    for tcg in range(8):
        qxg = qxg2[tcg % 2]
        for ic in range(16):
            op("scalar", lambda e, tcg=tcg, ic=ic, qxg=qxg: e.dma_start(
                out=qxg[:, ic * 256:(ic + 1) * 256],
                in_=xq_d[tcg * 256:(tcg + 1) * 256, ic * 128:(ic + 1) * 128], transpose=True),
               waits=[("D.mmlast.%d" % (tcg - 2)) if (ic == 0 and tcg >= 2) else None],
               inc="dQX%d" % (tcg % 2), inc_amt=16)
        mark("D.qx.%d" % tcg, "dQX%d" % (tcg % 2))
        for og in range(NG):
            wgt, wut = wgt2[og % 2], wut2[og % 2]
            if og >= 2:
                wdep = "D.wused.%d.%d" % (tcg, og - 2)
            elif tcg >= 1:
                prev_og = NG - 1 - ((NG - 1 - og) % 2)
                wdep = "D.wused.%d.%d" % (tcg - 1, prev_og)
            else:
                wdep = None
            op("sync", lambda e, og=og, wgt=wgt: e.dma_start(
                out=wgt[:].rearrange("p (ic c) -> p ic c", c=512),
                in_=wgT_d[:, og * 512:(og + 1) * 512].rearrange("(ic p) c -> p ic c", p=128)),
               waits=[wdep], inc="dWG%d" % (og % 2), inc_amt=16, name="D.wg.%d.%d" % (tcg, og))
            op("sync", lambda e, og=og, wut=wut: e.dma_start(
                out=wut[:].rearrange("p (ic c) -> p ic c", c=512),
                in_=wuT_d[:, og * 512:(og + 1) * 512].rearrange("(ic p) c -> p ic c", p=128)),
               inc="dWG%d" % (og % 2), inc_amt=16, name="D.wu.%d.%d" % (tcg, og))
            for tci in range(2):
                gbank, ubank = tci + (og % 2) * 4, 2 + tci + (og % 2) * 4
                if og >= 2:
                    gdep = "D.tt.%d.%d.%d" % (tcg, og - 2, tci)
                elif tcg >= 1:
                    prev_og2 = NG - 1 - ((NG - 1 - og) % 2)
                    gdep = "D.tt.%d.%d.%d" % (tcg - 1, prev_og2, tci)
                else:
                    gdep = None
                for ic in range(16):
                    op("tensor", lambda e, qxg=qxg, wgt=wgt, ic=ic, tci=tci, gbank=gbank: e.matmul(
                        psb[gbank], qxg[:, ic * 256 + tci * 128: ic * 256 + tci * 128 + 128],
                        wgt[:, ic * 512:(ic + 1) * 512], start=(ic == 0), stop=(ic == 15)),
                       waits=([("D.qx.%d" % tcg), ("D.wu.%d.%d" % (tcg, og)), gdep]
                              if ic == 0 else []),
                       inc="sT" if ic == 15 else None,
                       name=("D.gmm.%d.%d.%d" % (tcg, og, tci)) if ic == 15 else None)
                    op("tensor", lambda e, qxg=qxg, wut=wut, ic=ic, tci=tci, ubank=ubank: e.matmul(
                        psb[ubank], qxg[:, ic * 256 + tci * 128: ic * 256 + tci * 128 + 128],
                        wut[:, ic * 512:(ic + 1) * 512], start=(ic == 0), stop=(ic == 15)),
                       inc="sT" if ic == 15 else None,
                       name=("D.umm.%d.%d.%d" % (tcg, og, tci)) if ic == 15 else None)
                sg = sg2[tci + (og % 2) * 2]
                op("scalar", lambda e, sg=sg, gbank=gbank: e.activation(
                    out=sg[:], in_=psb[gbank], func=AF.Silu),
                   waits=["D.gmm.%d.%d.%d" % (tcg, og, tci), gdep],
                   inc="sS", name="D.sg.%d.%d.%d" % (tcg, og, tci))
                tr = trows[tci]
                ttw = ["D.umm.%d.%d.%d" % (tcg, og, tci), "D.sg.%d.%d.%d" % (tcg, og, tci)]
                if og == 0 and tcg >= 1:
                    ttw.append("D.qt.%d.%d" % (tcg - 1, tci))
                op("vector", lambda e, tr=tr, sg=sg, og=og, ubank=ubank: e.tensor_tensor(
                    out=tr[:, og * 512:(og + 1) * 512], in0=sg[:], in1=psb[ubank],
                    op=ALU.mult),
                   waits=ttw, inc="sV", name="D.tt.%d.%d.%d" % (tcg, og, tci))
                op("vector", lambda e, tr=tr, og=og, tci=tci: e.reduce_max(
                    mcols[:, tci * 16 + og: tci * 16 + og + 1], tr[:, og * 512:(og + 1) * 512],
                    axis=AX, apply_absolute_value=True),
                   waits=["D.tt.%d.%d.%d" % (tcg, og, tci)], inc="sV",
                   name="D.mx.%d.%d.%d" % (tcg, og, tci))
            mark("D.wused.%d.%d" % (tcg, og), "sT")
        mark("D.mmlast.%d" % tcg, "sT")
        for tci in range(2):
            tr = trows[tci]
            qtrow = qtr2[tci]
            op("vector", lambda e, tci=tci: e.reduce_max(
                qsm[:, tci * 4:tci * 4 + 1], mcols[:, tci * 16:tci * 16 + NG], axis=AX),
               waits=["D.mx.%d.%d.%d" % (tcg, NG - 1, tci)], inc="sV",
               name="D.m.%d.%d" % (tcg, tci))
            op("vector", lambda e, tci=tci: e.tensor_scalar_mul(
                qsm[:, tci * 4 + 1:tci * 4 + 2], qsm[:, tci * 4:tci * 4 + 1], 1.0 / 224.0),
               waits=["D.m.%d.%d" % (tcg, tci)], inc="sV", name="D.msc.%d.%d" % (tcg, tci))
            op("vector", lambda e, tci=tci: e.reciprocal(
                qsm[:, tci * 4 + 2:tci * 4 + 3], qsm[:, tci * 4 + 1:tci * 4 + 2]),
               waits=["D.msc.%d.%d" % (tcg, tci)], inc="sV", name="D.minv.%d.%d" % (tcg, tci))
            op("scalar", lambda e, tr=tr, tci=tci: e.activation(
                out=q8r[:], in_=tr[:], func=AF.Copy, scale=qsm[:, tci * 4 + 2:tci * 4 + 3]),
               waits=["D.minv.%d.%d" % (tcg, tci),
                      ("D.qt.%d.%d" % (tcg, tci - 1)) if tci >= 1 else
                      (("D.qt.%d.1" % (tcg - 1)) if tcg >= 1 else None)],
               inc="sS", name="D.q8.%d.%d" % (tcg, tci))
            op("scalar", lambda e, qtrow=qtrow, tci=tci: e.activation(
                out=qtrow[:], in_=q8r[:], func=AF.Copy, scale=qsm[:, tci * 4 + 1:tci * 4 + 2]),
               waits=["D.q8.%d.%d" % (tcg, tci),
                      ("D.qtw.%d.%d" % (tcg - 1, tci)) if tcg >= 1 else None],
               inc="sS", name="D.qt.%d.%d" % (tcg, tci))
            op("gpsimd", lambda e, tcg=tcg, tci=tci, qtrow=qtrow: e.dma_start(
                out=qt_d[tcg * 256 + tci * 128: tcg * 256 + (tci + 1) * 128, :], in_=qtrow[:]),
               waits=["D.qt.%d.%d" % (tcg, tci)], inc="dQW%d" % tci, inc_amt=16,
               name="D.qtw.%d.%d" % (tcg, tci))

    barrier("D")
    esD.close()

    esE = ExitStack(); esE.__enter__()
    qtt_sb = sbuf(esE, "qtt_sb", [128, 44 * 512], BF16)
    wdg_sb = sbuf(esE, "wdg_sb", [128, 11 * 2048], BF16)
    yac_sb = sbuf(esE, "yac_sb", [128, 4 * 2048], F32)
    gf2 = [sbuf(esE, "gf%d" % i, [128, 512], BF16) for i in range(2)]
    st16 = sbuf(esE, "st16", [128, 4 * 2048], BF16)
    xt16 = sbuf(esE, "xt16", [128, 4 * 2048], F32)
    tv2 = [sbuf(esE, "tv2%d" % i, [128, 512], F32) for i in range(2)]
    ot2 = [sbuf(esE, "ot2%d" % i, [128, 512], F32) for i in range(2)]

    NOC = I_FFN // 128
    for sup in range(4):
        for oc in range(NOC):
            op("scalar", lambda e, sup=sup, oc=oc: e.dma_start(
                out=qtt_sb[:, oc * 512:(oc + 1) * 512],
                in_=qt_d[sup * 512:(sup + 1) * 512, oc * 128:(oc + 1) * 128], transpose=True),
               waits=[("E.mmdone.%d" % (sup - 1)) if (oc == 0 and sup >= 1) else None],
               inc="dQT", inc_amt=16)
        mark("E.qtt.%d" % sup, "dQT")
        for tc in range(4):
            r0 = sup * 512 + tc * 128
            op("sync", lambda e, r0=r0, tc=tc: e.dma_start(
                out=st16[:, tc * 2048:(tc + 1) * 2048], in_=sat_d[r0:r0 + 128, :]),
               waits=[("E.cmb1.%d.3.3" % (sup - 1)) if (tc == 0 and sup >= 1) else None],
               inc="dCB0", inc_amt=16)
            op("sync", lambda e, r0=r0, tc=tc: e.dma_start(
                out=xt16[:, tc * 2048:(tc + 1) * 2048], in_=x_d[r0:r0 + 128, :]),
               inc="dCB1", inc_amt=16)
        cp["E.sx.%d" % sup] = ("dCB1", ctr["dCB1"])
        cp["E.sx0.%d" % sup] = ("dCB0", ctr["dCB0"])
        for ocg in range(4):
            for j in range(11):
                oc = ocg * 11 + j
                if j == 0:
                    if ocg >= 1:
                        wddep = "E.accv.%d.%d" % (sup, ocg - 1)
                    elif sup >= 1:
                        wddep = "E.accv.%d.3" % (sup - 1)
                    else:
                        wddep = None
                else:
                    wddep = None
                op("sync", lambda e, oc=oc, j=j: e.dma_start(
                    out=wdg_sb[:, j * 2048:(j + 1) * 2048], in_=wdT_d[oc * 128:(oc + 1) * 128, :]),
                   waits=[wddep], inc="dWD", inc_amt=16)
            mark("E.wd.%d.%d" % (sup, ocg), "dWD")
            for tc in range(4):
                fi = ocg * 4 + tc
                gfi = sup * 16 + fi
                if gfi >= 2:
                    pfi = fi - 2 if fi >= 2 else fi + 14
                    psup = sup if fi >= 2 else sup - 1
                    bdep = "E.vac.%d.%d.%d" % (psup, pfi // 4, pfi % 4)
                else:
                    bdep = None
                gb = (fi % 2) * 4
                for j in range(11):
                    for og in range(4):
                        op("tensor", lambda e, ocg=ocg, tc=tc, j=j, og=og, gb=gb: e.matmul(
                            psb[gb + og],
                            qtt_sb[:, (ocg * 11 + j) * 512 + tc * 128: (ocg * 11 + j) * 512 + tc * 128 + 128],
                            wdg_sb[:, j * 2048 + og * 512: j * 2048 + og * 512 + 512],
                            start=(j == 0), stop=(j == 10)),
                           waits=([("E.qtt.%d" % sup), ("E.wd.%d.%d" % (sup, ocg)), bdep]
                                  if (j == 0 and og == 0) else []),
                           inc="sT" if j == 10 else None,
                           name=("E.ymm.%d.%d.%d.%d" % (sup, ocg, tc, og)) if j == 10 else None)
                for og in range(4):
                    if ocg == 0:
                        yw = ["E.ymm.%d.%d.%d.%d" % (sup, ocg, tc, og)]
                        if sup >= 1:
                            yw.append("E.gfall.%d" % (sup - 1))
                        op("vector", lambda e, tc=tc, og=og, gb=gb: e.tensor_copy(
                            out=yac_sb[:, tc * 2048 + og * 512: tc * 2048 + og * 512 + 512],
                            in_=psb[gb + og]),
                           waits=yw, inc="sV", name="E.yac.%d.%d.%d.%d" % (sup, ocg, tc, og))
                    else:
                        op("vector", lambda e, tc=tc, og=og, gb=gb: e.tensor_tensor(
                            out=yac_sb[:, tc * 2048 + og * 512: tc * 2048 + og * 512 + 512],
                            in0=yac_sb[:, tc * 2048 + og * 512: tc * 2048 + og * 512 + 512],
                            in1=psb[gb + og], op=ALU.add),
                           waits=["E.ymm.%d.%d.%d.%d" % (sup, ocg, tc, og)],
                           inc="sV", name="E.yac.%d.%d.%d.%d" % (sup, ocg, tc, og))
                mark("E.vac.%d.%d.%d" % (sup, ocg, tc), "sV")
            mark("E.accv.%d.%d" % (sup, ocg), "sV")
        mark("E.mmdone.%d" % sup, "sT")
        for tc in range(4):
            for og in range(4):
                fl = tc * 4 + og
                pi = fl % 2
                gf, tv, ot = gf2[pi], tv2[pi], ot2[pi]
                r0 = sup * 512 + tc * 128
                if fl >= 2:
                    ptc, pog = divmod(fl - 2, 4)
                    bufdep = "E.cmb1.%d.%d.%d" % (sup, ptc, pog)
                elif sup >= 1:
                    bufdep = "E.cmb1.%d.3.%d" % (sup - 1, 2 + og)
                else:
                    bufdep = None
                op("scalar", lambda e, tc=tc, og=og, gf=gf: e.activation(
                    out=gf[:], in_=yac_sb[:, tc * 2048 + og * 512: tc * 2048 + og * 512 + 512],
                    func=AF.Silu),
                   waits=["E.accv.%d.3" % sup, bufdep],
                   inc="sS", name="E.gf.%d.%d.%d" % (sup, tc, og))
                op("vector", lambda e, gf=gf, tv=tv, tc=tc, og=og: e.tensor_tensor(
                    out=tv[:], in0=gf[:], in1=st16[:, tc * 2048 + og * 512: tc * 2048 + og * 512 + 512],
                    op=ALU.mult),
                   waits=["E.sx0.%d" % sup, "E.gf.%d.%d.%d" % (sup, tc, og)],
                   inc="sV", name="E.tv.%d.%d.%d" % (sup, tc, og))
                if fl >= 2:
                    podep = "E.od.%d.%d.%d" % (sup, ptc, pog)
                elif sup >= 1:
                    podep = "E.od.%d.3.%d" % (sup - 1, 2 + og)
                else:
                    podep = None
                op("vector", lambda e, tv=tv, ot=ot, tc=tc, og=og: e.tensor_tensor(
                    out=ot[:], in0=tv[:], in1=xt16[:, tc * 2048 + og * 512: tc * 2048 + og * 512 + 512],
                    op=ALU.add),
                   waits=["E.sx.%d" % sup, "E.tv.%d.%d.%d" % (sup, tc, og), podep],
                   inc="sV", name="E.cmb1.%d.%d.%d" % (sup, tc, og))
                op("sync", lambda e, r0=r0, og=og, ot=ot: e.dma_start(
                    out=out_d[r0:r0 + 128, og * 512:(og + 1) * 512], in_=ot[:]),
                   waits=["E.cmb1.%d.%d.%d" % (sup, tc, og)], inc="dO%d" % (fl % 2), inc_amt=16,
                   name="E.od.%d.%d.%d" % (sup, tc, og))
        mark("E.gfall.%d" % sup, "sS")

    op("sync", lambda e: None, waits=[("dO0", ctr["dO0"]), ("dO1", ctr["dO1"])])
    op("gpsimd", lambda e: None, waits=[("dQW0", ctr["dQW0"]), ("dQW1", ctr["dQW1"])])
    flush()
    esE.close()
    es_top.close()
    return nc


_NC_CACHE = None

def kernel(L0_attn_x, L0_attn_w_qkv, L0_attn_w_o, L0_ffn_w_gate, L0_ffn_w_up,
           L0_ffn_w_down, L0_ffn_up_reorder_index, L0_ffn_down_reorder_index):
    global _NC_CACHE
    x = np.ascontiguousarray(np.asarray(L0_attn_x, dtype=np.float32))
    w_qkv = np.ascontiguousarray(np.asarray(L0_attn_w_qkv, dtype=np.float32))
    w_o = np.ascontiguousarray(np.asarray(L0_attn_w_o, dtype=np.float32))
    w_g = np.ascontiguousarray(np.asarray(L0_ffn_w_gate, dtype=np.float32))
    w_u = np.ascontiguousarray(np.asarray(L0_ffn_w_up, dtype=np.float32))
    w_d = np.asarray(L0_ffn_w_down, dtype=np.float32)
    down_idx = np.asarray(L0_ffn_down_reorder_index)
    wd_perm = np.ascontiguousarray(w_d[:, np.argsort(down_idx)])
    wv = np.ascontiguousarray(w_qkv[:, 4096:6144])

    if _NC_CACHE is None:
        _NC_CACHE = build_nc()
    nc = _NC_CACHE

    in_maps = []
    for c in range(8):
        b, p = c // 2, c % 2
        xb = x[b]
        xrot = np.ascontiguousarray(np.concatenate([xb[S_OWN * p:], xb[:S_OWN * p]], axis=0)) if p else xb
        wqk = np.ascontiguousarray(np.concatenate(
            [w_qkv[:, 1024 * p:1024 * p + 1024],
             w_qkv[:, 2048 + 1024 * p:2048 + 1024 * p + 1024]], axis=1))
        S = np.zeros((128, 16), np.float32)
        for h in range(16):
            r_h = 2 * b + h // 8
            half = 0 if p == (r_h % 2) else 1
            S[16 * r_h + (h % 8) + 8 * half, h] = 1.0
        in_maps.append({
            "x": xrot, "wqk": wqk, "wv": wv, "wo": w_o,
            "wg": w_g, "wu": w_u, "wd": wd_perm, "sel": S,
        })
    res = run_bass_kernel_spmd(nc, in_maps, core_ids=list(range(8)))
    out = np.empty((4, S_FULL, D), np.float32)
    for c in range(8):
        b, p = c // 2, c % 2
        out[b, S_OWN * p:S_OWN * (p + 1)] = res.results[c]["out"]
    return out
